# version 01
# baseline (speedup 1.0000x reference)
import numpy as np
import ml_dtypes
from contextlib import ExitStack

import concourse.bass as bass
import concourse.mybir as mybir
from concourse.bass_utils import run_bass_kernel_spmd

F32 = mybir.dt.float32
BF16 = mybir.dt.bfloat16
AF = mybir.ActivationFunctionType
ALU = mybir.AluOpType
AX = mybir.AxisListType
BF16NP = ml_dtypes.bfloat16

ENGINES = ("sync", "tensor", "vector", "scalar", "gpsimd")


class Ctx:

    def __init__(self, nc):
        self.nc = nc
        self.es = ExitStack()
        self.ops = []
        self.sems = {}
        self.cnt = {}
        self.last_wait = {e: {} for e in ENGINES}
        self.wcount = {e: 0 for e in ENGINES}
        self.stream_wmark = {}
        for e in ENGINES:
            self.sems[e] = self.es.enter_context(nc.semaphore("sem_" + e))
            self.cnt[e] = 0
        self.dma_groups = {}

    def sbuf(self, name, shape, dtype):
        return self.es.enter_context(self.nc.sbuf_tensor(name, list(shape), dtype))

    def scope(self):
        from contextlib import contextmanager

        @contextmanager
        def _s():
            es = ExitStack()
            old = self.es
            self.es = es
            try:
                yield
            finally:
                self.es = old
                es.close()

        return _s()

    def join(self):
        toks = []
        for s, c in self.cnt.items():
            if c > 0:
                toks.append((s, c))
        for e in ENGINES:
            for t in toks:
                self._wait(e, t)

    def psum(self, name, shape, dtype=F32):
        return self.es.enter_context(self.nc.psum_tensor(name, list(shape), dtype))

    def _wait(self, eng, tok):
        if tok is None:
            return
        if isinstance(tok, tuple) and len(tok) and isinstance(tok[0], tuple):
            for t in tok:
                self._wait(eng, t)
            return
        if isinstance(tok, tuple) and (len(tok) != 2 or not isinstance(tok[0], str)):
            for t in tok:
                self._wait(eng, t)
            return
        if not (isinstance(tok, tuple) and len(tok) == 2):
            raise AssertionError(f"bad token {tok!r}")
        sem, val = tok
        if self.last_wait[eng].get(sem, -1) >= val:
            return
        self.last_wait[eng][sem] = val
        self.wcount[eng] += 1
        self.ops.append(("W", eng, sem, val))

    def op(self, eng, method, *args, waits=(), inc=False, **kw):
        for t in waits:
            self._wait(eng, t)
        tok = None
        if inc:
            self.cnt[eng] += 1
            v = self.cnt[eng]
            h = self.sems[eng]
            self.ops.append(
                ("O", eng, lambda E, m=method, a=args, k=kw, h=h: getattr(E, m)(*a, **k).then_inc(h, 1))
            )
            tok = (eng, v)
        else:
            self.ops.append(("O", eng, lambda E, m=method, a=args, k=kw: getattr(E, m)(*a, **k)))
        return tok

    def dma(self, queue, out, in_, waits=(), stream="0", **kw):
        had_waits = False
        for t in waits:
            if t is not None:
                had_waits = True
            self._wait(queue, t)
        dq = f"d{queue}:{stream}"
        if dq not in self.sems:
            self.sems[dq] = self.es.enter_context(
                self.nc.semaphore("sem_" + dq.replace(":", "_")))
            self.cnt[dq] = 0
        self.cnt[dq] += 16
        v = self.cnt[dq]
        h = self.sems[dq]
        if self.stream_wmark.get(dq) != self.wcount[queue]:
            had_waits = True
        self.stream_wmark[dq] = self.wcount[queue]
        self.dma_groups.setdefault(dq, []).append((v, had_waits))
        self.ops.append(
            ("O", queue, lambda E, o=out, i=in_, k=kw, h=h: E.dma_start(out=o, in_=i, **k).then_inc(h, 16))
        )
        return (dq, v)

    def finish(self):
        for dq, c in self.cnt.items():
            if dq.startswith("d") and ":" in dq and c > 0:
                q = dq[1:].split(":")[0]
                self._wait(q, (dq, c))

    def barrier_token(self, eng):
        if self.cnt[eng] == 0:
            return None
        return (eng, self.cnt[eng])

    def fence_engine(self, eng):
        return self.op(eng, "engine_nop" if eng in ("vector", "gpsimd") else "nop", inc=True)

    def build(self, block):
        remap = {}
        for dq, lst in self.dma_groups.items():
            ends = {}
            group_end = None
            for v, had_waits in reversed(lst):
                if group_end is None:
                    group_end = v
                ends[v] = group_end
                if had_waits:
                    group_end = None
            remap[dq] = ends
        by_eng = {e: [] for e in ENGINES}
        for ent in self.ops:
            if ent[0] == "W":
                _, eng, sem, val = ent
                if sem in remap:
                    val = remap[sem].get(val, val)
                h = self.sems[sem]
                by_eng[eng].append(lambda E, h=h, v=val: E.wait_ge(h, v))
            else:
                _, eng, fn = ent
                by_eng[eng].append(fn)
        for name in ENGINES:
            fns = by_eng[name]
            if not fns:
                continue
            dec = getattr(block, name)

            def section(E, fns=fns):
                for fn in fns:
                    fn(E)

            dec(section)

    def close(self):
        self.es.close()


def ceil_div(a, b):
    return (a + b - 1) // b



def emit_layernorm(ctx, x_sb, t_rows, D, xn_sb, scratch, x_tok=None, out_war=(),
                   eps=1e-5):
    m, var, rstd, xc = scratch["mean"], scratch["var"], scratch["rstd"], scratch["xc"]
    w = (x_tok,) if x_tok else ()
    t0 = ctx.op("vector", "reduce_sum", m[:t_rows, 0:1], x_sb, AX.X, waits=w, inc=True)
    t1 = ctx.op("vector", "tensor_scalar_mul", m[:t_rows, 0:1], m[:t_rows, 0:1], 1.0 / D,
                waits=(t0,), inc=True)
    war_xc = (scratch.get("war_sq"),) if scratch.get("war_sq") else ()
    tks = ctx.op("vector", "tensor_scalar_sub", xc[:t_rows, :D], x_sb, m[:t_rows, 0:1],
                 waits=war_xc + (t1,), inc=True)
    tsq = ctx.op(
        "scalar", "activation", scratch["sq"][:t_rows, :D], xc[:t_rows, :D], AF.Square,
        accum_out=var[:t_rows, 0:1], waits=(tks,) + tuple(out_war), inc=True,
    )
    scratch["war_sq"] = tsq
    war_rs = (scratch.get("war_rstd"),) if scratch.get("war_rstd") else ()
    war_rs = war_rs + ((scratch.get("eps_tok"),) if scratch.get("eps_tok") else ())
    trs = ctx.op("scalar", "activation", rstd[:t_rows, 0:1], var[:t_rows, 0:1], AF.Sqrt,
                 bias=scratch["eps"][:t_rows, 0:1], scale=1.0 / D,
                 waits=war_rs + (tsq,), inc=True)
    scratch["eps_tok"] = None
    trc = ctx.op("vector", "reciprocal", rstd[:t_rows, 0:1], rstd[:t_rows, 0:1],
                 waits=(trs,), inc=True)
    tok = ctx.op("vector", "tensor_scalar_mul", xn_sb, xc[:t_rows, :D],
                 rstd[:t_rows, 0:1], waits=out_war + (trc,), inc=True)
    scratch["war_rstd"] = tok
    return tok


class MM:

    pass


def emit_mm_accum(ctx, out_psum, lhsT_tiles, rhs_tiles, waits=(), inc_last=True):
    n = len(lhsT_tiles)
    tok = None
    for i in range(n):
        tok = ctx.op(
            "tensor", "matmul", out_psum, lhsT_tiles[i], rhs_tiles[i],
            start=(i == 0), stop=(i == n - 1),
            waits=waits if i == 0 else (),
            inc=(inc_last and i == n - 1),
        )
    return tok


def emit_transpose128(ctx, out_psum_bf16, in_sb_bf16, ident_bf16, waits=()):
    return ctx.op("tensor", "transpose", out_psum_bf16, in_sb_bf16, ident_bf16,
                  waits=waits, inc=True)




def np_bf16(x):
    return np.ascontiguousarray(x.astype(BF16NP))


def identity_bf16():
    return np.eye(128, dtype=BF16NP)



_NC_CACHE = {}


def _get_nc(name, builder):
    if name not in _NC_CACHE:
        _NC_CACHE[name] = builder()
    return _NC_CACHE[name]


def kernel(**inp):
    B, S, D = 4, 4096, 1024
    T = B * S
    f32 = np.float32
    x = np.ascontiguousarray(inp["L0_attn_x"], dtype=f32)
    zrow = np.zeros((1, D), f32)
    cA = l3_attn_consts()
    cL1 = l1_attn_consts()
    cL2 = l2_ffn_consts()
    ID, ID32 = cA["ident"], cA["ident32"]

    def attn_launch(nc_name, builder, x_full, per_core_extra):
        nc = _get_nc(nc_name, builder)
        maps = []
        for c in range(8):
            b, hh = c // 2, c % 2
            m = dict(x=np.ascontiguousarray(x_full[b]))
            m.update(per_core_extra(b, hh))
            maps.append(m)
        res = run_bass_kernel_spmd(nc, maps, core_ids=list(range(8)))
        ya = np.stack([res.results[2 * b]["y"] for b in range(B)])
        yb = np.stack([res.results[2 * b + 1]["y"] for b in range(B)])
        return ya, yb

    def ffn_launch(nc_name, builder, x_full, ya, yb, per_core_extra):
        nc = _get_nc(nc_name, builder)
        xf = x_full.reshape(T, D)
        yaf = ya.reshape(T, D)
        ybf = yb.reshape(T, D)
        maps = []
        for c in range(8):
            sl = slice(c * 2048, (c + 1) * 2048)
            m = dict(x=np.ascontiguousarray(xf[sl]),
                     ya=np.ascontiguousarray(yaf[sl]),
                     yb=np.ascontiguousarray(ybf[sl]))
            m.update(per_core_extra(c))
            maps.append(m)
        res = run_bass_kernel_spmd(nc, maps, core_ids=list(range(8)))
        out = np.concatenate([r["xo"] for r in res.results], axis=0)
        return out.reshape(B, S, D)

    q_w, k_w, v_w, o_w = (inp["L0_attn_q_w"], inp["L0_attn_k_w"],
                          inp["L0_attn_v_w"], inp["L0_attn_o_w"])
    aramps = [l0_attn_aramp(0), l0_attn_aramp(8)]

    def l0a_extra(b, hh):
        sl = slice(hh * 512, (hh + 1) * 512)
        return dict(wq=np.ascontiguousarray(q_w[:, sl]),
                    wk=np.ascontiguousarray(k_w[:, sl]),
                    wv=np.ascontiguousarray(v_w[:, sl]),
                    wo=np.ascontiguousarray(o_w[sl, :]),
                    b_eff=zrow, aramp=aramps[hh], cmask=cA["cmask"],
                    ident=ID, ident32=ID32)

    ya, yb = attn_launch("l0a", lambda: build_l0_attn(0), x, l0a_extra)

    ncA = _get_nc("l0r", build_l0_route)
    xf = x.reshape(T, D)
    yaf, ybf = ya.reshape(T, D), yb.reshape(T, D)
    mapsA = []
    for c in range(8):
        sl = slice(c * 2048, (c + 1) * 2048)
        mapsA.append(dict(x=np.ascontiguousarray(xf[sl]),
                          ya=np.ascontiguousarray(yaf[sl]),
                          yb=np.ascontiguousarray(ybf[sl]),
                          Wg=inp["L0_ffn_Wg"], ident=ID, ident32=ID32))
    resA = run_bass_kernel_spmd(ncA, mapsA, core_ids=list(range(8)))
    sel = np.concatenate([r["sel"] for r in resA.results], 0)
    wpre = np.concatenate([r["wpre"] for r in resA.results], 0)
    xmid = np.concatenate([r["xmid"] for r in resA.results], 0)

    ncB = _get_nc("l0k", build_l0_rank)
    ltri = np.tril(np.ones((128, 128), f32), -1).T.copy()
    resB = run_bass_kernel_spmd(ncB, [dict(sel=sel, wpre=wpre, ltri=ltri)],
                                core_ids=[0])
    wf = resB.results[0]["wf"]

    ncC = _get_nc("l0e", build_l0_expert)
    mapsC = []
    for c in range(8):
        sl = slice(c * 2048, (c + 1) * 2048)
        mapsC.append(dict(xmid=np.ascontiguousarray(xmid[sl]),
                          w=np.ascontiguousarray(wf[sl]),
                          W1=inp["L0_ffn_W1"], W2=inp["L0_ffn_W2"],
                          W3=inp["L0_ffn_W3"], Ws1=inp["L0_ffn_Ws1"],
                          Ws2=inp["L0_ffn_Ws2"], Ws3=inp["L0_ffn_Ws3"],
                          ident=ID))
    resC = run_bass_kernel_spmd(ncC, mapsC, core_ids=list(range(8)))
    x = np.concatenate([r["xo"] for r in resC.results], 0).reshape(B, S, D)

    w_qkv, b_qkv = inp["L1_attn_w_qkv"], inp["L1_attn_b_qkv"]
    w_out1, b_out1 = inp["L1_attn_w_out"], inp["L1_attn_b_out"]

    def l1a_extra(b, hh):
        sq = slice(hh * 512, (hh + 1) * 512)
        sk = slice(1024 + hh * 512, 1024 + (hh + 1) * 512)
        sv = slice(2048 + hh * 512, 2048 + (hh + 1) * 512)
        return dict(wq=np.ascontiguousarray(w_qkv[:, sq]),
                    wk=np.ascontiguousarray(w_qkv[:, sk]),
                    wv=np.ascontiguousarray(w_qkv[:, sv]),
                    wo=np.ascontiguousarray(w_out1[sq, :]),
                    bq=np.ascontiguousarray(b_qkv[sq].reshape(-1, 1)),
                    bk=np.ascontiguousarray(b_qkv[sk].reshape(-1, 1)),
                    bv=np.ascontiguousarray(b_qkv[sv].reshape(-1, 1)),
                    bv_row=np.ascontiguousarray(b_qkv[sv].reshape(1, -1)),
                    b_eff=(np.ascontiguousarray(b_out1.reshape(1, D))
                           if hh == 0 else zrow),
                    **cL1)

    ya, yb = attn_launch("l1a", build_l1_attn, x, l1a_extra)

    x = ffn_launch("l1f", build_l1_ffn, x, ya, yb,
                   lambda c: dict(w_router=inp["L1_ffn_w_router"],
                                  w_fc=inp["L1_ffn_w_fc"],
                                  w_proj=inp["L1_ffn_w_proj"],
                                  ident=ID, ident32=ID32))

    w_qkv2, w_out2, b_out2 = (inp["L2_attn_w_qkv"], inp["L2_attn_w_out"],
                              inp["L2_attn_b_out"])

    def l2a_extra(b, hh):
        sl = slice(hh * 256, (hh + 1) * 256)
        return dict(wq=np.ascontiguousarray(w_qkv2[0:512][sl]),
                    wk=np.ascontiguousarray(w_qkv2[512:1024][sl]),
                    wv=np.ascontiguousarray(w_qkv2[1024:1536][sl]),
                    wo=np.ascontiguousarray(w_out2[:, sl]),
                    b_eff=(np.ascontiguousarray(b_out2.reshape(1, D))
                           if hh == 0 else zrow),
                    ident=ID)

    ya, yb = attn_launch("l2a", build_l2_attn, x, l2a_extra)

    x = ffn_launch("l2f", build_l2_ffn, x, ya, yb,
                   lambda c: dict(wr=inp["L2_ffn_w_freq_real"],
                                  wi=inp["L2_ffn_w_freq_imag"],
                                  bias=np.ascontiguousarray(
                                      inp["L2_ffn_bias"].reshape(1, D)),
                                  **cL2))

    Wq3, Wk3, Wv3, Wo3 = (inp["L3_attn_Wq"], inp["L3_attn_Wk"],
                          inp["L3_attn_Wv"], inp["L3_attn_Wo"])

    def l3a_extra(b, hh):
        sl = slice(hh * 512, (hh + 1) * 512)
        return dict(wq=np.ascontiguousarray(Wq3[:, sl]),
                    wk=np.ascontiguousarray(Wk3[:, sl]),
                    wv=np.ascontiguousarray(Wv3[:, sl]),
                    wo=np.ascontiguousarray(Wo3[sl, :]),
                    b_eff=zrow, **cA)

    ya, yb = attn_launch("l3a", build_l3_attn, x, l3a_extra)

    x = ffn_launch(
        "l3f", build_l3_ffn, x, ya, yb,
        lambda c: dict(gate_w=inp["L3_ffn_gate_w"],
                       expert_bias=np.ascontiguousarray(
                           inp["L3_ffn_expert_bias"].reshape(1, 8)),
                       w_up=inp["L3_ffn_w_up"], w_down=inp["L3_ffn_w_down"],
                       shared_w_up=inp["L3_ffn_shared_w_up"],
                       shared_w_down=inp["L3_ffn_shared_w_down"],
                       ident=ID, ident32=ID32))

    return np.ascontiguousarray(x, dtype=f32)



def load_weight_bf16(ctx, dram, K, N, name, stage, stage_tok, queue="sync",
                     out_tiles=None, out_war=(), stream_prefix=None):
    nk = ceil_div(K, 128)
    tiles = []
    tok = None
    for k in range(nk):
        rows = min(128, K - k * 128)
        s = k % len(stage)
        ld = ctx.dma(queue, stage[s][:rows, :N], dram[k * 128 : k * 128 + rows, :],
                     waits=(stage_tok[s],) if stage_tok[s] else (),
                     stream=f"{stream_prefix or name}{s}")
        if out_tiles is None:
            t = ctx.sbuf(f"{name}_k{k}", (128, N), BF16)
        else:
            t = out_tiles[k]
        tok = ctx.op("vector", "tensor_copy", t[:rows, :N], stage[s][:rows, :N],
                     waits=(ld,) + tuple(out_war), inc=True)
        stage_tok[s] = tok
        tiles.append(t)
    return tiles, tok


class PsumSlot:

    def __init__(self, tile):
        self.tile = tile
        self.tok = None

    def war(self):
        if self.tok is None:
            return ()
        if isinstance(self.tok, tuple) and len(self.tok) and isinstance(
                self.tok[0], tuple):
            return tuple(self.tok)
        return (self.tok,)


def make_wrow_state(ctx, Cmax, name="wrow"):
    return {"bufs": [ctx.sbuf(f"{name}{i}", (128, Cmax), BF16) for i in range(2)],
            "toks": [None, None]}


def load_weight_T_bf16(ctx, dram, R, C, name, stage, stage_tok, ident, tslot,
                       queue="sync", negate=False, extra_waits=(), wrow_state=None,
                       out_tiles=None, out_war=(), stream_prefix=None):
    assert R % 128 == 0 and C % 128 == 0
    nr, nc_ = R // 128, C // 128
    if out_tiles is None:
        out_tiles = [ctx.sbuf(f"{name}_c{c}", (128, R), BF16) for c in range(nc_)]
    extra_waits = tuple(extra_waits) + tuple(out_war)
    if wrow_state is None:
        wrow_state = make_wrow_state(ctx, C, name + "_rowbf")
    wrow = wrow_state["bufs"]
    wrow_tok = wrow_state["toks"]
    tok = None
    for r in range(nr):
        s = r % len(stage)
        ld = ctx.dma(queue, stage[s][:, :C], dram[r * 128 : (r + 1) * 128, :],
                     waits=(stage_tok[s],) if stage_tok[s] else (),
                     stream=f"{stream_prefix or name}{s}")
        wb = r % len(wrow)
        cast = ctx.op("vector", "tensor_copy", wrow[wb][:, :C], stage[s][:, :C],
                      waits=(ld,) + ((wrow_tok[wb],) if wrow_tok[wb] else ()), inc=True)
        stage_tok[s] = cast
        last_tt = None
        for c in range(nc_):
            tt = ctx.op("tensor", "transpose", tslot.tile[:],
                        wrow[wb][:, c * 128 : (c + 1) * 128],
                        ident[:], waits=(cast,) + tslot.war() + tuple(extra_waits),
                        inc=True)
            if negate:
                ev = ctx.op("scalar", "activation",
                            out_tiles[c][:, r * 128 : (r + 1) * 128], tslot.tile[:],
                            AF.Copy, scale=-1.0, waits=(tt,), inc=True)
            else:
                ev = ctx.op("scalar", "copy",
                            out_tiles[c][:, r * 128 : (r + 1) * 128], tslot.tile[:],
                            waits=(tt,), inc=True)
            tslot.tok = ev
            last_tt = tt
            tok = ev
        wrow_tok[wb] = last_tt
    return out_tiles, tok


def emit_dense_mm(ctx, out_psums, lhsT_tiles, rhs_tiles, n_chunks, waits=(),
                  accumulate_into=False):
    tok = None
    nk = len(lhsT_tiles)
    for ci, (n0, nsz) in enumerate(n_chunks):
        for k in range(nk):
            tok = ctx.op(
                "tensor", "matmul", out_psums[ci],
                lhsT_tiles[k], rhs_tiles[k][:, n0 : n0 + nsz],
                start=(k == 0 and not accumulate_into), stop=(k == nk - 1),
                waits=waits if (k == 0 and ci == 0) else (),
                inc=(k == nk - 1),
            )
    return tok



class Res:
    def __init__(self, ctx, n_tiles, D, name="x"):
        self.ctx = ctx
        self.D = D
        self.n = n_tiles
        self.tiles = [ctx.sbuf(f"{name}{i}", (128, D), F32) for i in range(n_tiles)]
        self.tok = [None] * n_tiles


_LN_ID = [0]


def make_ln_scratch(ctx, D, eps=1e-5, sq=None):
    i = _LN_ID[0]
    _LN_ID[0] += 1
    s = {
        "mean": ctx.sbuf(f"ln_mean{i}", (128, 1), F32),
        "var": ctx.sbuf(f"ln_var{i}", (128, 1), F32),
        "rstd": ctx.sbuf(f"ln_rstd{i}", (128, 1), F32),
        "xc": ctx.sbuf(f"ln_xc{i}", (128, D), F32),
        "sq": (sq if sq is not None else ctx.sbuf(f"ln_sq{i}", (128, D), F32)),
        "eps": ctx.sbuf(f"ln_eps{i}", (128, 1), F32),
    }
    s["eps_tok"] = ctx.op("vector", "memset", s["eps"][:, 0:1], eps, inc=True)
    return s



def build_l2_ffn(debug=False):
    nc = bass.Bass("TRN2", target_bir_lowering=False)
    T, D = 2048, 1024
    NT, ND = T // 128, D // 128
    x_d = nc.dram_tensor("x", [T, D], F32, kind="ExternalInput").ap()
    ya_d = nc.dram_tensor("ya", [T, D], F32, kind="ExternalInput").ap()
    yb_d = nc.dram_tensor("yb", [T, D], F32, kind="ExternalInput").ap()
    wr_d = nc.dram_tensor("wr", [D, D], F32, kind="ExternalInput").ap()
    wi_d = nc.dram_tensor("wi", [D, D], F32, kind="ExternalInput").ap()
    fr_d = nc.dram_tensor("fr", [D, D], BF16, kind="ExternalInput").ap()
    nfi_d = nc.dram_tensor("nfi", [D, D], BF16, kind="ExternalInput").ap()
    gr_d = nc.dram_tensor("gr", [D, D], BF16, kind="ExternalInput").ap()
    gi_d = nc.dram_tensor("gi", [D, D], BF16, kind="ExternalInput").ap()
    ngi_d = nc.dram_tensor("ngi", [D, D], BF16, kind="ExternalInput").ap()
    bias_d = nc.dram_tensor("bias", [1, D], F32, kind="ExternalInput").ap()
    id_d = nc.dram_tensor("ident", [128, 128], BF16, kind="ExternalInput").ap()
    xo_d = nc.dram_tensor("xo", [T, D], F32, kind="ExternalOutput").ap()

    io = dict(x=x_d, ya=ya_d, yb=yb_d, wr=wr_d, wi=wi_d,
              fr=fr_d, nfi=nfi_d, gr=gr_d, gi=gi_d, ngi=ngi_d,
              bias=bias_d, ident=id_d, xo=xo_d)
    if debug:
        io["dbg_m"] = nc.dram_tensor("dbg_m", [D, D], F32, kind="ExternalOutput").ap()
        io["dbg_bias"] = nc.dram_tensor("dbg_bias", [128, D], F32, kind="ExternalOutput").ap()
        io["dbg_xn"] = nc.dram_tensor("dbg_xn", [128, D], F32, kind="ExternalOutput").ap()
    ctx = Ctx(nc)
    emit_l2_ffn_body(ctx, io)
    with nc.Block() as blk:
        ctx.build(blk)
    ctx.close()
    return nc


def emit_l2_ffn_body(ctx, io):
    T, D = 2048, 1024
    NT, ND = T // 128, D // 128

    ident = ctx.sbuf("ident_sb", (128, 128), BF16)
    tok_id = ctx.dma("sync", ident[:], io["ident"])

    m_tiles = [ctx.sbuf(f"m_{k}", (128, D), BF16) for k in range(ND)]
    tslot = PsumSlot(ctx.psum("tpsum", (128, 128), BF16))
    mm_ps = [PsumSlot(ctx.psum(f"mmps{i}", (128, 512), F32)) for i in range(2)]
    n_chunks = [(0, 512), (512, 512)]
    tok_m = [None] * ND
    with ctx.scope():
        stage2 = [ctx.sbuf("wstage0", (128, D), F32), ctx.sbuf("wstage1", (128, D), F32)]
        stage2_tok = [None, None]
        wrT, tok_wr = load_weight_T_bf16(ctx, io["wr"], D, D, "wrT", stage2, stage2_tok,
                                         ident, tslot, queue="gpsimd",
                                         extra_waits=(tok_id,))
        wiT, tok_wi = load_weight_T_bf16(ctx, io["wi"], D, D, "wiT", stage2, stage2_tok,
                                         ident, tslot, queue="gpsimd")
        g_tiles = {}
        g_tok = {}
        for nm in ("gr", "gi", "ngi", "fr", "nfi"):
            tiles = [ctx.sbuf(f"{nm}_{k}", (128, D), BF16) for k in range(ND)]
            tt = None
            for k in range(ND):
                tt = ctx.dma("sync", tiles[k][:], io[nm][k * 128 : (k + 1) * 128, :])
            g_tiles[nm] = tiles
            g_tok[nm] = tt

        ar = [ctx.sbuf(f"ar_{j}", (128, D), BF16) for j in range(ND)]
        ai = [ctx.sbuf(f"ai_{j}", (128, D), BF16) for j in range(ND)]
        wait_a = (tok_wr, tok_wi, g_tok["gr"], g_tok["gi"], g_tok["ngi"])
        tok_ar = [None] * ND
        tok_ai = [None] * ND
        for j in range(ND):
            lh_wr = [wrT[k][:, j * 128 : (j + 1) * 128] for k in range(ND)]
            lh_wi = [wiT[k][:, j * 128 : (j + 1) * 128] for k in range(ND)]
            for ci in range(2):
                t = emit_dense_mm(ctx, [mm_ps[0].tile[:, :512]], lh_wr + lh_wi,
                                  g_tiles["gr"] + g_tiles["ngi"],
                                  [n_chunks[ci]], waits=wait_a + mm_ps[0].war())
                tok_ar[j] = ctx.op("vector", "tensor_copy",
                                   ar[j][:, n_chunks[ci][0] : n_chunks[ci][0] + 512],
                                   mm_ps[0].tile[:, :512], waits=(t,), inc=True)
                mm_ps[0].tok = tok_ar[j]
                t = emit_dense_mm(ctx, [mm_ps[1].tile[:, :512]], lh_wr + lh_wi,
                                  g_tiles["gi"] + g_tiles["gr"],
                                  [n_chunks[ci]], waits=mm_ps[1].war())
                tok_ai[j] = ctx.op("scalar", "copy",
                                   ai[j][:, n_chunks[ci][0] : n_chunks[ci][0] + 512],
                                   mm_ps[1].tile[:, :512], waits=(t,), inc=True)
                mm_ps[1].tok = tok_ai[j]
        wait_m = tuple(tok_ar) + tuple(tok_ai) + (g_tok["fr"], g_tok["nfi"])
        for m in range(ND):
            lh_fr = [g_tiles["fr"][k][:, m * 128 : (m + 1) * 128] for k in range(ND)]
            lh_nfi = [g_tiles["nfi"][k][:, m * 128 : (m + 1) * 128] for k in range(ND)]
            for ci in range(2):
                t = emit_dense_mm(ctx, [mm_ps[0].tile[:, :512]], lh_fr + lh_nfi,
                                  ar + ai, [n_chunks[ci]], waits=wait_m + mm_ps[0].war())
                tok_m[m] = ctx.op("vector", "tensor_copy",
                                  m_tiles[m][:, n_chunks[ci][0] : n_chunks[ci][0] + 512],
                                  mm_ps[0].tile[:, :512], waits=(t,), inc=True)
                mm_ps[0].tok = tok_m[m]
    ctx.join()

    bias_sb1 = ctx.sbuf("bias1", (1, D), F32)
    tok_b = ctx.dma("sync", bias_sb1[0:1, :], io["bias"])
    ones1 = ctx.sbuf("ones1", (1, 128), F32)
    tok_o = ctx.op("vector", "memset", ones1[0:1, :], 1.0, inc=True)
    bias_bc = ctx.sbuf("bias_bc", (128, D), F32)
    tok_bb = []
    for ci in range(2):
        t = ctx.op("tensor", "matmul", mm_ps[ci].tile[:, :512], ones1[0:1, :],
                   bias_sb1[0:1, ci * 512 : (ci + 1) * 512], start=True, stop=True,
                   waits=(tok_b, tok_o) + mm_ps[ci].war(), inc=True)
        tev = ctx.op("scalar", "copy", bias_bc[:, ci * 512 : (ci + 1) * 512],
                     mm_ps[ci].tile[:, :512], waits=(t,), inc=True)
        mm_ps[ci].tok = tev
        tok_bb.append(tev)

    lns = make_ln_scratch(ctx, D)
    xt = [ctx.sbuf(f"xt{i}", (128, D), F32) for i in range(2)]
    xs = [ctx.sbuf(f"xs{i}", (128, D), F32) for i in range(2)]
    xn = ctx.sbuf("xn", (128, D), BF16)
    xnT = [ctx.sbuf(f"xnT{k}", (128, 128), BF16) for k in range(ND)]
    y_ps = [PsumSlot(ctx.psum(f"yps{i}", (128, 512), F32)) for i in range(2)]
    xpb = ctx.sbuf("xpb", (128, D), F32)
    xout = ctx.sbuf("xout", (128, D), F32)
    prev = {}
    xt_tok = [None, None]
    xs_tok = [None, None]
    last_st = None
    for i in range(NT):
        r = slice(i * 128, (i + 1) * 128)
        s = i % 2
        tx = ctx.dma("sync", xt[s][:], io["x"][r, :],
                     waits=(xt_tok[s],) if xt_tok[s] else (), stream=f"px{s}")
        ta = ctx.dma("gpsimd", xs[s][:], io["ya"][r, :],
                     waits=(xs_tok[s],) if xs_tok[s] else (), stream=f"ps{s}")
        t1 = ctx.op("vector", "tensor_add", xt[s][:], xt[s][:], xs[s][:],
                    waits=(tx, ta), inc=True)
        tb = ctx.dma("gpsimd", xs[s][:], io["yb"][r, :], waits=(t1,), stream=f"ps{s}")
        t2 = ctx.op("vector", "tensor_add", xt[s][:], xt[s][:], xs[s][:],
                    waits=(tb, t1), inc=True)
        xs_tok[s] = t2
        xn_war = tuple(prev.get(("ev", k)) for k in range(ND) if ("ev", k) in prev)
        tok_ln = emit_layernorm(ctx, xt[s][:], 128, D, xn[:, :D], lns,
                                x_tok=t2, out_war=xn_war)
        ttoks = []
        for k in range(ND):
            tt = ctx.op("tensor", "transpose", tslot.tile[:],
                        xn[:, k * 128 : (k + 1) * 128], ident[:],
                        waits=(tok_ln,) + tslot.war(), inc=True)
            ev = ctx.op("scalar", "copy", xnT[k][:], tslot.tile[:], waits=(tt,), inc=True)
            tslot.tok = ev
            prev[("ev", k)] = ev
            ttoks.append(ev)
        t_xpb = ctx.op("vector", "tensor_add", xpb[:], xt[s][:], bias_bc[:],
                       waits=tuple(tok_bb)
                       + ((prev.get("xpb"),) if prev.get("xpb") else ()),
                       inc=True)
        xt_tok[s] = t_xpb
        for ci in range(2):
            t = emit_dense_mm(ctx, [y_ps[ci].tile[:, :512]], [xnT[k][:] for k in range(ND)],
                              m_tiles, [n_chunks[ci]],
                              waits=tuple(ttoks) + tuple(tok_m) + y_ps[ci].war())
            tev = ctx.op("vector", "tensor_add", xout[:, ci * 512 : (ci + 1) * 512],
                         y_ps[ci].tile[:, :512], xpb[:, ci * 512 : (ci + 1) * 512],
                         waits=(t, t_xpb) + ((last_st,) if last_st else ()),
                         inc=True)
            y_ps[ci].tok = tev
            prev[("y", ci)] = tev
        prev["xpb"] = prev[("y", 1)]
        if i == 0 and "dbg_xn" in io:
            dxn = ctx.sbuf("dbg_xn_sb", (128, D), F32)
            c1 = ctx.op("vector", "tensor_copy", dxn[:, :D], xn[:, :D],
                        waits=(tok_ln,), inc=True)
            ctx.dma("sync", io["dbg_xn"], dxn[:, :D], waits=(c1,))
        last_st = ctx.dma("sync", io["xo"][i * 128 : (i + 1) * 128, :], xout[:],
                          waits=(prev[("y", 0)], prev[("y", 1)]), stream="out")
    if "dbg_m" in io:
        dbg_sb = ctx.sbuf("dbg_sb", (128, D), F32)
        dt = None
        for k in range(ND):
            cp = ctx.op("vector", "tensor_copy", dbg_sb[:, :D], m_tiles[k][:, :D],
                        waits=(tok_m[k],) + ((dt,) if dt else ()), inc=True)
            dt = ctx.dma("sync", io["dbg_m"][k * 128 : (k + 1) * 128, :], dbg_sb[:, :D],
                         waits=(cp,))
        ctx.dma("sync", io["dbg_bias"], bias_bc[:, :D], waits=tuple(tok_bb))
    ctx.finish()



class TilePro:

    def __init__(self, ctx, D, ident, ident_tok, with_partials=True, name="tp",
                 fp32_T=False, ident32=None, ident32_tok=None, tslot=None,
                 tslot32=None, n_xt=2, direct32=False, ident_is32=False):
        self.D = D
        self.ND = D // 128
        self.ident = ident
        self.ident_tok = ident_tok
        self.with_partials = with_partials
        self.fp32_T = fp32_T
        self.ident32 = ident32
        self.ident32_tok = ident32_tok
        self.nxt = n_xt
        self.xt = [ctx.sbuf(f"{name}_xt{i}", (128, D), F32) for i in range(n_xt)]
        self.xs = ([ctx.sbuf(f"{name}_xs{i}", (128, D), F32) for i in range(2)]
                   if with_partials else None)
        self.direct32 = direct32
        self.ident_is32 = ident_is32
        self.xn32 = ctx.sbuf(f"{name}_xn32", (128, D), F32)
        self.xn = None if direct32 else ctx.sbuf(f"{name}_xn", (128, D), BF16)
        self.xnT = [ctx.sbuf(f"{name}_xnT{k}", (128, 128), BF16) for k in range(self.ND)]
        if fp32_T:
            self.xnT32 = [ctx.sbuf(f"{name}_xnT32_{k}", (128, 128), F32)
                          for k in range(self.ND)]
            self.tslot32 = tslot32 if tslot32 is not None else PsumSlot(
                ctx.psum(f"{name}_tps32", (128, 128), F32))
        self.lns = make_ln_scratch(ctx, D, sq=self.xn32)
        self.tslot = tslot if tslot is not None else PsumSlot(
            ctx.psum(f"{name}_tps", (128, 128), BF16))
        self.xt_tok = [None, None]
        self.xs_tok = [None, None]
        self.prev_ev = {}
        self.prev_ev32 = {}
        self.prev_cast = None

    def emit(self, ctx, i, x_d, ya_d, yb_d, row0=None):
        D, ND = self.D, self.ND
        r0 = (i * 128) if row0 is None else row0
        r = slice(r0, r0 + 128)
        s = i % self.nxt
        tx = ctx.dma("sync", self.xt[s][:], x_d[r, :],
                     waits=(self.xt_tok[s],) if self.xt_tok[s] else (),
                     stream=f"tpx{s}")
        t_mid = tx
        if self.with_partials:
            ta = ctx.dma("gpsimd", self.xs[s][:], ya_d[r, :],
                         waits=(self.xs_tok[s],) if self.xs_tok[s] else (),
                         stream=f"tps{s}")
            t1 = ctx.op("vector", "tensor_add", self.xt[s][:], self.xt[s][:],
                        self.xs[s][:], waits=(tx, ta), inc=True)
            tb = ctx.dma("gpsimd", self.xs[s][:], yb_d[r, :], waits=(t1,),
                         stream=f"tps{s}")
            t2 = ctx.op("vector", "tensor_add", self.xt[s][:], self.xt[s][:],
                        self.xs[s][:], waits=(tb, t1), inc=True)
            self.xs_tok[s] = t2
            t_mid = t2
        xn32_war = tuple(t for t in (self.prev_ev32.get(k) for k in range(ND)) if t)
        if self.prev_cast:
            xn32_war = xn32_war + (self.prev_cast,)
        t_ln = emit_layernorm(ctx, self.xt[s][:], 128, D, self.xn32[:, :D], self.lns,
                              x_tok=t_mid, out_war=xn32_war)
        evs = []
        if self.direct32:
            t_cast = t_ln
            for k in range(ND):
                tt = ctx.op("tensor", "transpose", self.tslot.tile[:],
                            self.xn32[:, k * 128 : (k + 1) * 128], self.ident[:],
                            waits=(t_ln, self.ident_tok) + self.tslot.war(),
                            inc=True)
                ev = ctx.op("scalar", "copy", self.xnT[k][:], self.tslot.tile[:],
                            waits=(tt,), inc=True)
                self.tslot.tok = ev
                self.prev_ev[k] = ev
                evs.append(ev)
            self.prev_ev32 = dict(enumerate(evs))
        else:
            xn_war = tuple(t for t in (self.prev_ev.get(k) for k in range(ND)) if t)
            t_cast = ctx.op("vector", "tensor_copy", self.xn[:, :D],
                            self.xn32[:, :D], waits=(t_ln,) + xn_war, inc=True)
            self.prev_cast = t_cast
            for k in range(ND):
                tt = ctx.op("tensor", "transpose", self.tslot.tile[:],
                            self.xn[:, k * 128 : (k + 1) * 128], self.ident[:],
                            waits=(t_cast, self.ident_tok) + self.tslot.war(),
                            inc=True)
                ev = ctx.op("scalar", "copy", self.xnT[k][:], self.tslot.tile[:],
                            waits=(tt,), inc=True)
                self.tslot.tok = ev
                self.prev_ev[k] = ev
                evs.append(ev)
        evs32 = []
        if self.fp32_T:
            for k in range(ND):
                tt = ctx.op("tensor", "transpose", self.tslot32.tile[:],
                            self.xn32[:, k * 128 : (k + 1) * 128], self.ident32[:],
                            waits=(t_ln, self.ident32_tok) + self.tslot32.war(),
                            inc=True)
                ev = ctx.op("scalar", "copy", self.xnT32[k][:], self.tslot32.tile[:],
                            waits=(tt,), inc=True)
                self.tslot32.tok = ev
                self.prev_ev32[k] = ev
                evs32.append(ev)
        return dict(xt=self.xt[s], s=s, t_mid=t_mid, t_ln=t_ln, evs=evs,
                    evs32=evs32, t_cast=t_cast)

    def set_xt_done(self, s, tok):
        self.xt_tok[s] = tok



def build_l1_ffn():
    nc = bass.Bass("TRN2", target_bir_lowering=False)
    T, D, E, H = 2048, 1024, 8, 512
    x_d = nc.dram_tensor("x", [T, D], F32, kind="ExternalInput").ap()
    ya_d = nc.dram_tensor("ya", [T, D], F32, kind="ExternalInput").ap()
    yb_d = nc.dram_tensor("yb", [T, D], F32, kind="ExternalInput").ap()
    wr_d = nc.dram_tensor("w_router", [E, D], F32, kind="ExternalInput").ap()
    wfc_d = nc.dram_tensor("w_fc", [E, H, D], F32, kind="ExternalInput").ap()
    wpj_d = nc.dram_tensor("w_proj", [E, D, H], F32, kind="ExternalInput").ap()
    id_d = nc.dram_tensor("ident", [128, 128], BF16, kind="ExternalInput").ap()
    id32_d = nc.dram_tensor("ident32", [128, 128], F32, kind="ExternalInput").ap()
    xo_d = nc.dram_tensor("xo", [T, D], F32, kind="ExternalOutput").ap()
    ctx = Ctx(nc)
    emit_l1_ffn_body(ctx, dict(x=x_d, ya=ya_d, yb=yb_d, w_router=wr_d, w_fc=wfc_d,
                               w_proj=wpj_d, ident=id_d, ident32=id32_d, xo=xo_d))
    with nc.Block() as blk:
        ctx.build(blk)
    ctx.close()
    return nc


def emit_router_T32(ctx, wr_dram, E, D, ident32, tok_id32, tslot32, stage, name="wrT32"):
    ND = D // 128
    rtmp = ctx.sbuf(name + "_rtmp", (128, D), F32)
    tz = ctx.op("vector", "memset", rtmp[:, :D], 0.0, inc=True)
    tr = ctx.dma("sync", rtmp[:E, :D], wr_dram, waits=(tz,))
    wrT = [ctx.sbuf(f"{name}_{k}", (128, E), F32) for k in range(ND)]
    toks = []
    for k in range(ND):
        tt = ctx.op("tensor", "transpose", tslot32.tile[:],
                    rtmp[:, k * 128 : (k + 1) * 128], ident32[:],
                    waits=(tz, tr, tok_id32) + tslot32.war(), inc=True)
        ev = ctx.op("scalar", "copy", wrT[k][:, :E], tslot32.tile[:, :E], waits=(tt,),
                    inc=True)
        tslot32.tok = ev
        toks.append(ev)
    return wrT, toks


def emit_l1_ffn_body(ctx, io):
    T, D, E, H = 2048, 1024, 8, 512
    NT, ND, NH = T // 128, D // 128, H // 128

    ident = ctx.sbuf("ident_sb", (128, 128), BF16)
    tok_id = ctx.dma("sync", ident[:], io["ident"])
    ident32 = ctx.sbuf("ident32_sb", (128, 128), F32)
    tok_id32 = ctx.dma("sync", ident32[:], io["ident32"])

    stage = [ctx.sbuf("wstage0", (128, D), F32), ctx.sbuf("wstage1", (128, D), F32)]
    stok = [None, None]
    tslot = PsumSlot(ctx.psum("wtps", (128, 128), BF16))
    tslot32 = PsumSlot(ctx.psum("wtps32", (128, 128), F32))
    wrT, tok_wr = emit_router_T32(ctx, io["w_router"], E, D, ident32, tok_id32,
                                  tslot32, stage)
    wrs = make_wrow_state(ctx, D)
    wfcT = []
    wpjT = []
    tok_w = []
    for e in range(E):
        t_f, tok_f = load_weight_T_bf16(ctx, io["w_fc"][e], H, D, f"wfcT{e}",
                                        stage, stok, ident, tslot, queue="gpsimd",
                                        wrow_state=wrs)
        t_p, tok_p = load_weight_T_bf16(ctx, io["w_proj"][e], D, H, f"wpjT{e}",
                                        stage, stok, ident, tslot, queue="gpsimd",
                                        wrow_state=wrs)
        wfcT.append(t_f)
        wpjT.append(t_p)
        tok_w.append((tok_f, tok_p))

    tp = TilePro(ctx, D, ident, tok_id, with_partials=True, fp32_T=True,
                 ident32=ident32, ident32_tok=tok_id32)
    logit_ps = PsumSlot(ctx.psum("logit_ps", (128, 8), F32))
    h_ps = PsumSlot(ctx.psum("h_ps", (128, H), F32))
    y_ps = [PsumSlot(ctx.psum(f"y_ps{i}", (128, 512), F32)) for i in range(2)]
    lg = ctx.sbuf("lg", (128, 8), F32)
    mx = ctx.sbuf("mx", (128, 1), F32)
    mask = ctx.sbuf("mask", (128, 8), F32)
    habs = ctx.sbuf("habs", (128, H), F32)
    hleak = ctx.sbuf("hleak", (128, H), F32)
    g = ctx.sbuf("g_sb", (128, H), BF16)
    gT_all = [ctx.sbuf(f"gTa{j}", (128, 128), BF16) for j in range(E * NH)]
    xout = ctx.sbuf("xout", (128, D), F32)
    prev = {}
    last_st = None
    for i in range(NT):
        pro = tp.emit(ctx, i, io["x"], io["ya"], io["yb"])
        xnT = tp.xnT
        t = emit_dense_mm(ctx, [logit_ps.tile[:, :8]], [x[:] for x in tp.xnT32], wrT,
                          [(0, 8)], waits=tuple(pro["evs32"]) + tuple(tok_wr)
                          + logit_ps.war())
        t_lg = ctx.op("vector", "tensor_copy", lg[:, :8], logit_ps.tile[:, :8],
                      waits=(t,) + ((prev.get("lg"),) if prev.get("lg") else ()),
                      inc=True)
        logit_ps.tok = t_lg
        t_mx = ctx.op("vector", "reduce_max", mx[:, 0:1], lg[:, :8], AX.X,
                      waits=(t_lg,), inc=True)
        t_mask = ctx.op("vector", "tensor_scalar", mask[:, :8], lg[:, :8],
                        mx[:, 0:1], None, ALU.is_ge,
                        waits=(t_mx,) + ((prev.get("mask"),) if prev.get("mask") else ()),
                        inc=True)
        prev["lg"] = t_mask
        g_toks = []
        for e in range(E):
            t_h = emit_dense_mm(ctx, [h_ps.tile[:, :H]], [x[:] for x in xnT],
                                wfcT[e], [(0, H)],
                                waits=(tuple(pro["evs"]) + tok_w[e][0:1]
                                       + h_ps.war()))
            t_abs = ctx.op("scalar", "activation", habs[:, :H], h_ps.tile[:, :H],
                           AF.Abs, scale=0.25,
                           waits=(t_h,) + ((prev.get("habs"),) if prev.get("habs") else ()),
                           inc=True)
            t_lk = ctx.op("vector", "scalar_tensor_tensor", hleak[:, :H],
                          h_ps.tile[:, :H], 0.75, habs[:, :H], ALU.mult, ALU.add,
                          waits=(t_h, t_abs) + ((prev.get("hleak"),) if prev.get("hleak") else ()),
                          inc=True)
            h_ps.tok = t_lk
            prev["habs"] = t_lk
            t_g = ctx.op("scalar", "activation", g[:, :H], hleak[:, :H], AF.Square,
                         scale=mask[:, e : e + 1],
                         waits=(t_lk, t_mask) + ((prev.get("g"),) if prev.get("g") else ()),
                         inc=True)
            prev["hleak"] = t_g
            for k in range(NH):
                tt = ctx.op("tensor", "transpose", tslot.tile[:],
                            g[:, k * 128 : (k + 1) * 128], ident[:],
                            waits=(t_g,) + tslot.war(), inc=True)
                gev = ctx.op("scalar", "copy", gT_all[e * NH + k][:], tslot.tile[:],
                             waits=(tt,), inc=True)
                tslot.tok = gev
                g_toks.append(gev)
            prev["g"] = g_toks[-1]
        prev["mask"] = prev["hleak"]
        wtoks = tuple(tok_w[e][1] for e in range(E))
        for ci in range(2):
            tmm = None
            for e in range(E):
                for k in range(NH):
                    first = e == 0 and k == 0
                    last = e == E - 1 and k == NH - 1
                    tmm = ctx.op(
                        "tensor", "matmul", y_ps[ci].tile[:, :512],
                        gT_all[e * NH + k][:],
                        wpjT[e][k][:, ci * 512 : (ci + 1) * 512],
                        start=first, stop=last,
                        waits=(tuple(g_toks) + wtoks + y_ps[ci].war()) if first else (),
                        inc=last,
                    )
            prev[("ymm", ci)] = tmm
        for ci in range(2):
            tev = ctx.op("vector", "tensor_add", xout[:, ci * 512 : (ci + 1) * 512],
                         y_ps[ci].tile[:, :512],
                         pro["xt"][:, ci * 512 : (ci + 1) * 512],
                         waits=(prev[("ymm", ci)], pro["t_mid"])
                         + ((last_st,) if last_st else ()),
                         inc=True)
            y_ps[ci].tok = tev
            prev[("y", ci)] = tev
        tp.set_xt_done(pro["s"], prev[("y", 1)])
        last_st = ctx.dma("sync", io["xo"][i * 128 : (i + 1) * 128, :], xout[:],
                          waits=(prev[("y", 0)], prev[("y", 1)]), stream="out")
    ctx.finish()



def build_l3_ffn():
    nc = bass.Bass("TRN2", target_bir_lowering=False)
    T, D, E, HID = 2048, 1024, 8, 1408
    x_d = nc.dram_tensor("x", [T, D], F32, kind="ExternalInput").ap()
    ya_d = nc.dram_tensor("ya", [T, D], F32, kind="ExternalInput").ap()
    yb_d = nc.dram_tensor("yb", [T, D], F32, kind="ExternalInput").ap()
    gw_d = nc.dram_tensor("gate_w", [E, D], F32, kind="ExternalInput").ap()
    eb_d = nc.dram_tensor("expert_bias", [1, E], F32, kind="ExternalInput").ap()
    wu_d = nc.dram_tensor("w_up", [E, HID, D], F32, kind="ExternalInput").ap()
    wd_d = nc.dram_tensor("w_down", [E, D, HID], F32, kind="ExternalInput").ap()
    su_d = nc.dram_tensor("shared_w_up", [HID, D], F32, kind="ExternalInput").ap()
    sd_d = nc.dram_tensor("shared_w_down", [D, HID], F32, kind="ExternalInput").ap()
    id_d = nc.dram_tensor("ident", [128, 128], BF16, kind="ExternalInput").ap()
    id32_d = nc.dram_tensor("ident32", [128, 128], F32, kind="ExternalInput").ap()
    xo_d = nc.dram_tensor("xo", [T, D], F32, kind="ExternalOutput").ap()
    ctx = Ctx(nc)
    emit_l3_ffn_body(ctx, dict(x=x_d, ya=ya_d, yb=yb_d, gate_w=gw_d, expert_bias=eb_d,
                               w_up=wu_d, w_down=wd_d, shared_w_up=su_d,
                               shared_w_down=sd_d, ident=id_d, ident32=id32_d,
                               xo=xo_d))
    with nc.Block() as blk:
        ctx.build(blk)
    ctx.close()
    return nc


def emit_l3_ffn_body(ctx, io):
    T, D, E, HID = 2048, 1024, 8, 1408
    NT, ND, NJ = T // 128, D // 128, HID // 128

    ident = ctx.sbuf("ident_sb", (128, 128), BF16)
    tok_id = ctx.dma("sync", ident[:], io["ident"])
    ident32 = ctx.sbuf("ident32_sb", (128, 128), F32)
    tok_id32 = ctx.dma("sync", ident32[:], io["ident32"])

    tslot = PsumSlot(ctx.psum("tslot_bf", (128, 128), BF16))
    tslot32 = PsumSlot(ctx.psum("tslot_32", (128, 128), F32))
    logit_ps = PsumSlot(ctx.psum("logit_ps", (128, 128), F32))
    hT_ps = PsumSlot(ctx.psum("hT_ps", (128, HID), F32))
    y_ps = [PsumSlot(ctx.psum(f"y_ps{i}", (128, 512), F32)) for i in range(2)]

    stage = [ctx.sbuf("wstage0", (128, HID), F32)]
    stok = [None]
    wrs = {"bufs": [ctx.sbuf("wrow0", (128, HID), BF16)], "toks": [None]}
    xnT_all = [[ctx.sbuf(f"xnTa_{i}_{k}", (128, 128), BF16) for k in range(ND)]
               for i in range(NT)]
    xnT_tok = [None] * NT
    wsq_all = ctx.sbuf("wsq_all", (128, NT * 8), F32)
    racc = [ctx.sbuf(f"racc{i}", (128, D), F32) for i in range(NT)]
    racc_tok = [None] * NT
    wup_buf = [ctx.sbuf(f"wup_b{c}", (128, HID), BF16) for c in range(ND)]
    wdn_buf = [ctx.sbuf(f"wdn_b{c}", (128, D), BF16) for c in range(NJ)]
    gbf = ctx.sbuf("gbf", (128, HID), BF16)
    rtmp = ctx.sbuf("rtmp_bf", (128, D), BF16)
    xstage = ctx.sbuf("xstage", (128, D), F32)
    eb1 = ctx.sbuf("eb1", (1, E), F32)
    ones1 = ctx.sbuf("ones1", (1, 128), F32)
    eb_bc = ctx.sbuf("eb_bc", (128, E), F32)
    prev = {}
    wsq_tok = None

    with ctx.scope():
        tp = TilePro(ctx, D, ident, tok_id, with_partials=True, fp32_T=True,
                     ident32=ident32, ident32_tok=tok_id32, tslot=tslot,
                     tslot32=tslot32)
        gwT, tok_gw = emit_router_T32(ctx, io["gate_w"], E, D, ident32, tok_id32,
                                      tslot32, stage, name="gwT32")
        tok_eb1 = ctx.dma("sync", eb1[0:1, :], io["expert_bias"])
        tok_o = ctx.op("vector", "memset", ones1[0:1, :], 1.0, inc=True)
        t = ctx.op("tensor", "matmul", logit_ps.tile[:, :E], ones1[0:1, :],
                   eb1[0:1, :E], start=True, stop=True,
                   waits=(tok_eb1, tok_o) + logit_ps.war(), inc=True)
        tok_eb = ctx.op("scalar", "copy", eb_bc[:, :E], logit_ps.tile[:, :E],
                        waits=(t,), inc=True)
        logit_ps.tok = tok_eb
        lg = ctx.sbuf("lg", (128, E), F32)
        sc = ctx.sbuf("sc", (128, E), F32)
        bi = ctx.sbuf("bi", (128, E), F32)
        mx = ctx.sbuf("mx", (128, 2), F32)
        sel = ctx.sbuf("sel", (128, E), F32)
        hit = ctx.sbuf("hit", (128, E), F32)
        for i in range(NT):
            pro = tp.emit(ctx, i, io["x"], io["ya"], io["yb"])
            tp.set_xt_done(pro["s"], pro["t_ln"])
            for k in range(ND):
                xnT_tok[i] = ctx.op("vector", "tensor_copy", xnT_all[i][k][:],
                                    tp.xnT[k][:], waits=(pro["evs"][k],), inc=True)
            t = emit_dense_mm(ctx, [logit_ps.tile[:, :E]],
                              [x[:] for x in tp.xnT32], gwT, [(0, E)],
                              waits=tuple(pro["evs32"]) + tuple(tok_gw)
                              + logit_ps.war())
            t_lg = ctx.op("vector", "tensor_copy", lg[:, :E], logit_ps.tile[:, :E],
                          waits=(t,) + ((prev.get("lg"),) if prev.get("lg") else ()),
                          inc=True)
            logit_ps.tok = t_lg
            t_e = ctx.op("scalar", "activation", sc[:, :E], lg[:, :E], AF.Exp,
                         scale=-1.0,
                         waits=(t_lg,) + ((prev.get("sc"),) if prev.get("sc") else ()),
                         inc=True)
            t_e = ctx.op("vector", "tensor_scalar_add", sc[:, :E], sc[:, :E], 1.0,
                         waits=(t_e,), inc=True)
            t_sc = ctx.op("vector", "reciprocal", sc[:, :E], sc[:, :E], waits=(t_e,),
                          inc=True)
            t_bi = ctx.op("vector", "tensor_add", bi[:, :E], sc[:, :E], eb_bc[:, :E],
                          waits=(t_sc, tok_eb)
                          + ((prev.get("bi"),) if prev.get("bi") else ()), inc=True)
            t_m1 = ctx.op("vector", "reduce_max", mx[:, 0:1], bi[:, :E], AX.X,
                          waits=(t_bi,), inc=True)
            t_h1 = ctx.op("vector", "tensor_scalar", hit[:, :E], bi[:, :E],
                          mx[:, 0:1], None, ALU.is_ge,
                          waits=(t_m1,) + ((prev.get("hit"),) if prev.get("hit") else ()),
                          inc=True)
            t_b2 = ctx.op("vector", "scalar_tensor_tensor", bi[:, :E], hit[:, :E],
                          -1e30, bi[:, :E], ALU.mult, ALU.add, waits=(t_h1,),
                          inc=True)
            t_m2 = ctx.op("vector", "reduce_max", mx[:, 1:2], bi[:, :E], AX.X,
                          waits=(t_b2,), inc=True)
            t_h2 = ctx.op("vector", "tensor_scalar", sel[:, :E], bi[:, :E],
                          mx[:, 1:2], None, ALU.is_ge,
                          waits=(t_m2,) + ((prev.get("sel"),) if prev.get("sel") else ()),
                          inc=True)
            t_sel = ctx.op("vector", "tensor_add", sel[:, :E], sel[:, :E],
                           hit[:, :E], waits=(t_h2, t_h1), inc=True)
            prev["hit"] = t_sel
            prev["bi"] = t_m2
            wq = wsq_all[:, i * 8 : i * 8 + E]
            t_w = ctx.op("vector", "tensor_mul", wq, sc[:, :E], sel[:, :E],
                         waits=(t_sel,), inc=True)
            t_w = ctx.op("vector", "tensor_mul", wq, wq, wq, waits=(t_w,), inc=True)
            prev["sc"] = t_w
            prev["lg"] = t_w
            prev["sel"] = t_w
            racc_tok[i] = ctx.op("vector", "memset", racc[i][:, :D], 0.0, inc=True)
        wsq_tok = prev["sc"]
    ctx.join()

    wup_war, wdn_war = (), ()

    def expert_pass(tok_u, tok_d, e):
        nonlocal prev
        last_up = None
        last_dn = None
        for i in range(NT):
            if e is None:
                r = slice(i * 128, (i + 1) * 128)
                war = (prev.get("xstage"),) if prev.get("xstage") else ()
                tx = ctx.dma("sync", xstage[:], io["x"][r, :], waits=war,
                             stream="xst")
                t1 = ctx.op("vector", "tensor_add", racc[i][:, :D], racc[i][:, :D],
                            xstage[:], waits=(tx, racc_tok[i]), inc=True)
                tx = ctx.dma("sync", xstage[:], io["ya"][r, :], waits=(t1,),
                             stream="xst")
                t1 = ctx.op("vector", "tensor_add", racc[i][:, :D], racc[i][:, :D],
                            xstage[:], waits=(tx, t1), inc=True)
                tx = ctx.dma("sync", xstage[:], io["yb"][r, :], waits=(t1,),
                             stream="xst")
                t1 = ctx.op("vector", "tensor_add", racc[i][:, :D], racc[i][:, :D],
                            xstage[:], waits=(tx, t1), inc=True)
                prev["xstage"] = t1
                racc_tok[i] = t1
            tmm = None
            for j in range(NJ):
                for k in range(ND):
                    tmm = ctx.op(
                        "tensor", "matmul",
                        hT_ps.tile[:, j * 128 : (j + 1) * 128],
                        wup_buf[k][:, j * 128 : (j + 1) * 128], xnT_all[i][k][:],
                        start=(k == 0), stop=(k == ND - 1),
                        waits=((xnT_tok[i], tok_u) + hT_ps.war())
                        if (j == 0 and k == 0) else (),
                        inc=(j == NJ - 1 and k == ND - 1),
                    )
            last_up = tmm
            t_r = ctx.op("scalar", "activation", gbf[:, :HID], hT_ps.tile[:, :HID],
                         AF.Relu,
                         waits=(tmm,) + ((prev.get("gbf"),) if prev.get("gbf") else ()),
                         inc=True)
            t_g = ctx.op("vector", "tensor_mul", gbf[:, :HID], gbf[:, :HID],
                         gbf[:, :HID], waits=(t_r,), inc=True)
            hT_ps.tok = t_g
            prev["gbf"] = t_g
            for ci in range(2):
                tmm = None
                for j in range(NJ):
                    tmm = ctx.op(
                        "tensor", "matmul", y_ps[ci].tile[:, :512],
                        gbf[:, j * 128 : (j + 1) * 128],
                        wdn_buf[j][:, ci * 512 : (ci + 1) * 512],
                        start=(j == 0), stop=(j == NJ - 1),
                        waits=((t_g, tok_d) + y_ps[ci].war()) if j == 0 else (),
                        inc=(j == NJ - 1),
                    )
                last_dn = tmm
                if e is not None:
                    t_c = ctx.op("vector", "tensor_scalar_mul",
                                 rtmp[:, ci * 512 : (ci + 1) * 512],
                                 y_ps[ci].tile[:, :512],
                                 wsq_all[:, i * 8 + e : i * 8 + e + 1],
                                 waits=(tmm, wsq_tok)
                                 + ((prev.get(("rt", ci)),) if prev.get(("rt", ci)) else ()),
                                 inc=True)
                    y_ps[ci].tok = t_c
                    t_a = ctx.op("vector", "tensor_add",
                                 racc[i][:, ci * 512 : (ci + 1) * 512],
                                 racc[i][:, ci * 512 : (ci + 1) * 512],
                                 rtmp[:, ci * 512 : (ci + 1) * 512],
                                 waits=(t_c, racc_tok[i]), inc=True)
                    prev[("rt", ci)] = t_a
                    racc_tok[i] = t_a
                else:
                    t2 = ctx.op("vector", "tensor_add",
                                racc[i][:, ci * 512 : (ci + 1) * 512],
                                racc[i][:, ci * 512 : (ci + 1) * 512],
                                y_ps[ci].tile[:, :512],
                                waits=(tmm, racc_tok[i]), inc=True)
                    y_ps[ci].tok = t2
                    racc_tok[i] = t2
                    prev[("y", ci)] = t2
            if e is None:
                st = ctx.dma("sync", io["xo"][i * 128 : (i + 1) * 128, :],
                             racc[i][:, :D],
                             waits=(prev[("y", 0)], prev[("y", 1)]), stream="out")
                racc_tok[i] = st
        return last_up, last_dn

    for e in range(E):
        up_t, tok_u = load_weight_T_bf16(
            ctx, io["w_up"][e], HID, D, f"wupT{e}", stage, stok, ident, tslot,
            queue="gpsimd", wrow_state=wrs, extra_waits=(tok_id,),
            out_tiles=wup_buf, out_war=wup_war, stream_prefix="wup")
        dn_t, tok_d = load_weight_T_bf16(
            ctx, io["w_down"][e], D, HID, f"wdnT{e}", stage, stok, ident, tslot,
            queue="gpsimd", wrow_state=wrs, out_tiles=wdn_buf, out_war=wdn_war,
            stream_prefix="wdn")
        lu, ld = expert_pass(tok_u, tok_d, e)
        wup_war = (lu,)
        wdn_war = (ld,)
    up_t, tok_u = load_weight_T_bf16(
        ctx, io["shared_w_up"], HID, D, "suT", stage, stok, ident, tslot,
        queue="gpsimd", wrow_state=wrs, out_tiles=wup_buf, out_war=wup_war,
        stream_prefix="wup")
    dn_t, tok_d = load_weight_T_bf16(
        ctx, io["shared_w_down"], D, HID, "sdT", stage, stok, ident, tslot,
        queue="gpsimd", wrow_state=wrs, out_tiles=wdn_buf, out_war=wdn_war,
        stream_prefix="wdn")
    expert_pass(tok_u, tok_d, None)
    ctx.finish()



def build_l0_route():
    nc = bass.Bass("TRN2", target_bir_lowering=False)
    T, D, E = 2048, 1024, 32
    NT, ND = T // 128, D // 128
    x_d = nc.dram_tensor("x", [T, D], F32, kind="ExternalInput").ap()
    ya_d = nc.dram_tensor("ya", [T, D], F32, kind="ExternalInput").ap()
    yb_d = nc.dram_tensor("yb", [T, D], F32, kind="ExternalInput").ap()
    wg_d = nc.dram_tensor("Wg", [D, E], F32, kind="ExternalInput").ap()
    id_d = nc.dram_tensor("ident", [128, 128], BF16, kind="ExternalInput").ap()
    id32_d = nc.dram_tensor("ident32", [128, 128], F32, kind="ExternalInput").ap()
    sel_d = nc.dram_tensor("sel", [T, E], F32, kind="ExternalOutput").ap()
    wpre_d = nc.dram_tensor("wpre", [T, E], F32, kind="ExternalOutput").ap()
    xmid_d = nc.dram_tensor("xmid", [T, D], F32, kind="ExternalOutput").ap()

    ctx = Ctx(nc)
    ident = ctx.sbuf("ident_sb", (128, 128), BF16)
    tok_id = ctx.dma("sync", ident[:], id_d)
    ident32 = ctx.sbuf("ident32_sb", (128, 128), F32)
    tok_id32 = ctx.dma("sync", ident32[:], id32_d)
    tp = TilePro(ctx, D, ident, tok_id, with_partials=True, fp32_T=True,
                 ident32=ident32, ident32_tok=tok_id32)
    wg = [ctx.sbuf(f"wg{k}", (128, E), F32) for k in range(ND)]
    tok_wg = None
    for k in range(ND):
        tok_wg = ctx.dma("sync", wg[k][:, :E], wg_d[k * 128 : (k + 1) * 128, :])
    logit_ps = PsumSlot(ctx.psum("logit_ps", (128, E), F32))
    lg = ctx.sbuf("lg", (128, E), F32)
    sc = ctx.sbuf("sc", (128, E), F32)
    scr = ctx.sbuf("scr", (128, E), F32)
    hit = ctx.sbuf("hit", (128, E), F32)
    sel = ctx.sbuf("sel_sb", (128, E), F32)
    wpre = ctx.sbuf("wpre_sb", (128, E), F32)
    mx = ctx.sbuf("mx_sb", (128, 2), F32)
    prev = {}
    for i in range(NT):
        pro = tp.emit(ctx, i, x_d, ya_d, yb_d)
        st_x = ctx.dma("gpsimd", xmid_d[i * 128 : (i + 1) * 128, :], pro["xt"][:],
                       waits=(pro["t_mid"],))
        tp.set_xt_done(pro["s"], st_x)
        t = emit_dense_mm(ctx, [logit_ps.tile[:, :E]], [x[:] for x in tp.xnT32], wg,
                          [(0, E)], waits=tuple(pro["evs32"]) + (tok_wg,)
                          + logit_ps.war())
        t_lg = ctx.op("vector", "tensor_copy", lg[:, :E], logit_ps.tile[:, :E],
                      waits=(t,) + ((prev.get("lg"),) if prev.get("lg") else ()),
                      inc=True)
        logit_ps.tok = t_lg
        t_m = ctx.op("vector", "reduce_max", mx[:, 0:1], lg[:, :E], AX.X,
                     waits=(t_lg,), negate=True, inc=True)
        t_e = ctx.op("scalar", "activation", sc[:, :E], lg[:, :E], AF.Exp,
                     bias=mx[:, 0:1], accum_out=mx[:, 1:2],
                     waits=(t_m,) + ((prev.get("sc"),) if prev.get("sc") else ()),
                     inc=True)
        t_r = ctx.op("vector", "reciprocal", mx[:, 1:2], mx[:, 1:2], waits=(t_e,),
                     inc=True)
        t_sc = ctx.op("vector", "tensor_scalar_mul", sc[:, :E], sc[:, :E],
                      mx[:, 1:2], waits=(t_r,), inc=True)
        t_cp = ctx.op("vector", "tensor_copy", scr[:, :E], lg[:, :E],
                      waits=(t_lg,) + ((prev.get("scr"),) if prev.get("scr") else ()),
                      inc=True)
        t_z = ctx.op("vector", "memset", sel[:, :E], 0.0,
                     waits=((prev.get("sel"),) if prev.get("sel") else ()), inc=True)
        tprev = (t_cp, t_z)
        for r4 in range(4):
            t_m4 = ctx.op("vector", "reduce_max", mx[:, 0:1], scr[:, :E], AX.X,
                          waits=tprev, inc=True)
            t_h = ctx.op("vector", "tensor_scalar", hit[:, :E], scr[:, :E],
                         mx[:, 0:1], None, ALU.is_ge,
                         waits=(t_m4,) + ((prev.get("hit"),) if prev.get("hit") else ()),
                         inc=True)
            prev["hit"] = None
            t_s = ctx.op("vector", "tensor_add", sel[:, :E], sel[:, :E], hit[:, :E],
                         waits=(t_h,), inc=True)
            t_sub = ctx.op("vector", "scalar_tensor_tensor", scr[:, :E], hit[:, :E],
                           -1e30, scr[:, :E], ALU.mult, ALU.add, waits=(t_h,),
                           inc=True)
            tprev = (t_s, t_sub)
            prev["hit"] = t_sub
        t_w = ctx.op("vector", "tensor_mul", wpre[:, :E], sc[:, :E], sel[:, :E],
                     waits=(t_sc, tprev[0])
                     + ((prev.get("wpre"),) if prev.get("wpre") else ()), inc=True)
        prev["sc"] = t_w
        prev["lg"] = t_w
        prev["scr"] = tprev[1]
        st1 = ctx.dma("sync", sel_d[i * 128 : (i + 1) * 128, :], sel[:, :E],
                      waits=(tprev[0],))
        st2 = ctx.dma("sync", wpre_d[i * 128 : (i + 1) * 128, :], wpre[:, :E],
                      waits=(t_w,))
        prev["sel"] = st1
        prev["wpre"] = st2
    ctx.finish()
    with nc.Block() as blk:
        ctx.build(blk)
    ctx.close()
    return nc


def build_l0_rank():
    nc = bass.Bass("TRN2", target_bir_lowering=False)
    T, E, C = 16384, 32, 8192
    NT = T // 128
    sel_d = nc.dram_tensor("sel", [T, E], F32, kind="ExternalInput").ap()
    wpre_d = nc.dram_tensor("wpre", [T, E], F32, kind="ExternalInput").ap()
    ltri_d = nc.dram_tensor("ltri", [128, 128], F32, kind="ExternalInput").ap()
    wf_d = nc.dram_tensor("wf", [T, E], F32, kind="ExternalOutput").ap()
    ctx = Ctx(nc)
    ltri = ctx.sbuf("ltri_sb", (128, 128), F32)
    tok_l = ctx.dma("sync", ltri[:], ltri_d)
    ones1 = ctx.sbuf("ones1", (1, 128), F32)
    onesc = ctx.sbuf("onesc", (128, 1), F32)
    t_o1 = ctx.op("vector", "memset", ones1[0:1, :], 1.0, inc=True)
    t_oc = ctx.op("vector", "memset", onesc[:, 0:1], 1.0, inc=True)
    offs = ctx.sbuf("offs", (1, E), F32)
    t_off = ctx.op("vector", "memset", offs[0:1, :E], 0.0, inc=True)
    selt = [ctx.sbuf(f"selt{i}", (128, E), F32) for i in range(2)]
    wt = [ctx.sbuf(f"wt{i}", (128, E), F32) for i in range(2)]
    rank = ctx.sbuf("rank", (128, E), F32)
    tot = ctx.sbuf("tot", (1, E), F32)
    rank_ps = PsumSlot(ctx.psum("rank_ps", (128, E), F32))
    tot_ps = PsumSlot(ctx.psum("tot_ps", (1, E), F32))
    sel_tok = [None, None]
    wt_tok = [None, None]
    prev = {}
    for i in range(NT):
        s = i % 2
        r = slice(i * 128, (i + 1) * 128)
        t_sl = ctx.dma("sync", selt[s][:, :E], sel_d[r, :],
                       waits=(sel_tok[s],) if sel_tok[s] else ())
        t_wl = ctx.dma("gpsimd", wt[s][:, :E], wpre_d[r, :],
                       waits=(wt_tok[s],) if wt_tok[s] else ())
        t1 = ctx.op("tensor", "matmul", rank_ps.tile[:, :E], ltri[:], selt[s][:, :E],
                    start=True, stop=False, waits=(tok_l, t_sl, t_off) + rank_ps.war())
        t2 = ctx.op("tensor", "matmul", rank_ps.tile[:, :E], ones1[0:1, :],
                    offs[0:1, :E], start=False, stop=True, inc=True)
        t3 = ctx.op("tensor", "matmul", tot_ps.tile[0:1, :E], onesc[:, 0:1],
                    selt[s][:, :E], start=True, stop=True, waits=(t_oc,) + tot_ps.war(),
                    inc=True)
        t_cm = ctx.op("vector", "tensor_scalar", rank[:, :E], rank_ps.tile[:, :E],
                      float(C), None, ALU.is_lt,
                      waits=(t2,) + ((prev.get("rank"),) if prev.get("rank") else ()),
                      inc=True)
        rank_ps.tok = t_cm
        t_wf = ctx.op("vector", "tensor_mul", wt[s][:, :E], wt[s][:, :E], rank[:, :E],
                      waits=(t_cm, t_wl), inc=True)
        prev["rank"] = t_wf
        st = ctx.dma("sync", wf_d[r, :], wt[s][:, :E], waits=(t_wf,))
        wt_tok[s] = st
        t_ev = ctx.op("scalar", "copy", tot[0:1, :E], tot_ps.tile[0:1, :E],
                      waits=(t3,) + ((prev.get("tot"),) if prev.get("tot") else ()),
                      inc=True)
        tot_ps.tok = t_ev
        t_off = ctx.op("vector", "tensor_add", offs[0:1, :E], offs[0:1, :E],
                       tot[0:1, :E], waits=(t_ev, t_off), inc=True)
        prev["tot"] = t_off
        sel_tok[s] = t3
    ctx.finish()
    with nc.Block() as blk:
        ctx.build(blk)
    ctx.close()
    return nc


def build_l0_expert():
    nc = bass.Bass("TRN2", target_bir_lowering=False)
    T, D, E, H = 2048, 1024, 32, 512
    NT, ND, NJ = T // 128, D // 128, H // 128
    x_d = nc.dram_tensor("xmid", [T, D], F32, kind="ExternalInput").ap()
    w_d = nc.dram_tensor("w", [T, E], F32, kind="ExternalInput").ap()
    w1_d = nc.dram_tensor("W1", [E, D, H], F32, kind="ExternalInput").ap()
    w2_d = nc.dram_tensor("W2", [E, H, D], F32, kind="ExternalInput").ap()
    w3_d = nc.dram_tensor("W3", [E, D, H], F32, kind="ExternalInput").ap()
    ws1_d = nc.dram_tensor("Ws1", [D, D], F32, kind="ExternalInput").ap()
    ws2_d = nc.dram_tensor("Ws2", [D, D], F32, kind="ExternalInput").ap()
    ws3_d = nc.dram_tensor("Ws3", [D, D], F32, kind="ExternalInput").ap()
    id_d = nc.dram_tensor("ident", [128, 128], BF16, kind="ExternalInput").ap()
    xo_d = nc.dram_tensor("xo", [T, D], F32, kind="ExternalOutput").ap()
    ctx = Ctx(nc)
    emit_l0_expert_body(ctx, dict(xmid=x_d, w=w_d, W1=w1_d, W2=w2_d, W3=w3_d,
                                  Ws1=ws1_d, Ws2=ws2_d, Ws3=ws3_d, ident=id_d,
                                  xo=xo_d))
    with nc.Block() as blk:
        ctx.build(blk)
    ctx.close()
    return nc


def emit_l0_expert_body(ctx, io):
    T, D, E, H = 2048, 1024, 32, 512
    NT, ND, NJ = T // 128, D // 128, H // 128
    HS = 1024
    NJS = HS // 128

    ident = ctx.sbuf("ident_sb", (128, 128), BF16)
    tok_id = ctx.dma("sync", ident[:], io["ident"])
    tslot = PsumSlot(ctx.psum("tslot_bf", (128, 128), BF16))
    h1_ps = PsumSlot(ctx.psum("h1_ps", (128, HS), F32))
    h3_ps = PsumSlot(ctx.psum("h3_ps", (128, HS), F32))
    y_ps = [PsumSlot(ctx.psum(f"y_ps{i}", (128, 512), F32)) for i in range(2)]

    xnT_all = [[ctx.sbuf(f"xnTa_{i}_{k}", (128, 128), BF16) for k in range(ND)]
               for i in range(NT)]
    xnT_tok = [None] * NT
    w_all = ctx.sbuf("w_all", (128, NT * E), F32)
    w_tok = [None] * NT
    racc = [ctx.sbuf(f"racc{i}", (128, D), F32) for i in range(NT)]
    racc_tok = [None] * NT
    w1b = [ctx.sbuf(f"w1b{k}", (128, HS), BF16) for k in range(ND)]
    w3b = [ctx.sbuf(f"w3b{k}", (128, HS), BF16) for k in range(ND)]
    w2b = [ctx.sbuf(f"w2b{j}", (128, D), BF16) for j in range(NJS)]
    stage = [ctx.sbuf("wstage0", (128, HS), F32), ctx.sbuf("wstage1", (128, HS), F32)]
    stok = [None, None]
    s1 = ctx.sbuf("s1_sb", (128, HS), BF16)
    gbf = ctx.sbuf("gbf", (128, HS), BF16)
    xstage = ctx.sbuf("xstage", (128, D), F32)
    prev = {}

    with ctx.scope():
        tp = TilePro(ctx, D, ident, tok_id, with_partials=False, tslot=tslot)
        for i in range(NT):
            pro = tp.emit(ctx, i, io["xmid"], None, None)
            tp.set_xt_done(pro["s"], pro["t_ln"])
            for k in range(ND):
                xnT_tok[i] = ctx.op("vector", "tensor_copy", xnT_all[i][k][:],
                                    tp.xnT[k][:], waits=(pro["evs"][k],), inc=True)
            w_tok[i] = ctx.dma("sync", w_all[:, i * E : (i + 1) * E],
                               io["w"][i * 128 : (i + 1) * 128, :])
            racc_tok[i] = ctx.op("vector", "memset", racc[i][:, :D], 0.0, inc=True)
    ctx.join()

    w1_war = w3_war = w2_war = ()

    def swiglu_pass(tok_w1, tok_w3, tok_w2, e, HH):
        nonlocal prev
        NJH = HH // 128
        last_up = None
        last_dn = None
        for i in range(NT):
            if e is None:
                r = slice(i * 128, (i + 1) * 128)
                war = (prev.get("xstage"),) if prev.get("xstage") else ()
                tx = ctx.dma("sync", xstage[:], io["xmid"][r, :], waits=war,
                             stream="xst")
                t1 = ctx.op("vector", "tensor_add", racc[i][:, :D], racc[i][:, :D],
                            xstage[:], waits=(tx, racc_tok[i]), inc=True)
                prev["xstage"] = t1
                racc_tok[i] = t1
            for hp, wb, tk in ((h1_ps, w1b, tok_w1), (h3_ps, w3b, tok_w3)):
                tmm = None
                for j in range(NJH):
                    for k in range(ND):
                        tmm = ctx.op(
                            "tensor", "matmul",
                            hp.tile[:, j * 128 : (j + 1) * 128],
                            wb[k][:, j * 128 : (j + 1) * 128], xnT_all[i][k][:],
                            start=(k == 0), stop=(k == ND - 1),
                            waits=((xnT_tok[i], tk) + hp.war())
                            if (j == 0 and k == 0) else (),
                            inc=(j == NJH - 1 and k == ND - 1),
                        )
                if hp is h1_ps:
                    last_up = tmm
                    t_h1 = tmm
                else:
                    t_h3 = tmm
            t_s = ctx.op("scalar", "activation", s1[:, :HH], h1_ps.tile[:, :HH],
                         AF.Silu,
                         waits=(t_h1,) + ((prev.get("s1"),) if prev.get("s1") else ()),
                         inc=True)
            h1_ps.tok = t_s
            t_g = ctx.op("vector", "tensor_tensor", gbf[:, :HH], s1[:, :HH],
                         h3_ps.tile[:, :HH], ALU.mult,
                         waits=(t_s, t_h3) + ((prev.get("gbf"),) if prev.get("gbf") else ()),
                         inc=True)
            h3_ps.tok = t_g
            prev["s1"] = t_g
            prev["gbf"] = t_g
            for ci in range(2):
                tmm = None
                for j in range(NJH):
                    tmm = ctx.op(
                        "tensor", "matmul", y_ps[ci].tile[:, :512],
                        gbf[:, j * 128 : (j + 1) * 128],
                        w2b[j][:, ci * 512 : (ci + 1) * 512],
                        start=(j == 0), stop=(j == NJH - 1),
                        waits=((t_g, tok_w2) + y_ps[ci].war()) if j == 0 else (),
                        inc=(j == NJH - 1),
                    )
                last_dn = tmm
                if e is not None:
                    t_a = ctx.op("vector", "scalar_tensor_tensor",
                                 racc[i][:, ci * 512 : (ci + 1) * 512],
                                 y_ps[ci].tile[:, :512],
                                 w_all[:, i * E + e : i * E + e + 1],
                                 racc[i][:, ci * 512 : (ci + 1) * 512],
                                 ALU.mult, ALU.add,
                                 waits=(tmm, w_tok[i], racc_tok[i]), inc=True)
                else:
                    t_a = ctx.op("vector", "tensor_add",
                                 racc[i][:, ci * 512 : (ci + 1) * 512],
                                 racc[i][:, ci * 512 : (ci + 1) * 512],
                                 y_ps[ci].tile[:, :512],
                                 waits=(tmm, racc_tok[i]), inc=True)
                y_ps[ci].tok = t_a
                racc_tok[i] = t_a
                prev[("y", ci)] = t_a
            if e is None:
                st = ctx.dma("sync", io["xo"][i * 128 : (i + 1) * 128, :],
                             racc[i][:, :D],
                             waits=(prev[("y", 0)], prev[("y", 1)]), stream="out")
                racc_tok[i] = st
        return last_up, last_dn

    for e in range(E):
        _, tok_w1 = load_weight_bf16(ctx, io["W1"][e], D, H, f"w1_{e}", stage,
                                     stok, queue="gpsimd", out_tiles=w1b,
                                     out_war=w1_war, stream_prefix="wa")
        _, tok_w3 = load_weight_bf16(ctx, io["W3"][e], D, H, f"w3_{e}", stage,
                                     stok, queue="gpsimd", out_tiles=w3b,
                                     out_war=w3_war, stream_prefix="wa")
        _, tok_w2 = load_weight_bf16(ctx, io["W2"][e], H, D, f"w2_{e}", stage,
                                     stok, queue="gpsimd", out_tiles=w2b,
                                     out_war=w2_war, stream_prefix="wa")
        lu, ld = swiglu_pass(tok_w1, tok_w3, tok_w2, e, H)
        w1_war = w3_war = (lu,)
        w2_war = (ld,)
    _, tok_w1 = load_weight_bf16(ctx, io["Ws1"], D, HS, "ws1", stage, stok,
                                 queue="gpsimd", out_tiles=w1b, out_war=w1_war,
                                 stream_prefix="wa")
    _, tok_w3 = load_weight_bf16(ctx, io["Ws3"], D, HS, "ws3", stage, stok,
                                 queue="gpsimd", out_tiles=w3b, out_war=w3_war,
                                 stream_prefix="wa")
    _, tok_w2 = load_weight_bf16(ctx, io["Ws2"], HS, D, "ws2", stage, stok,
                                 queue="gpsimd", out_tiles=w2b, out_war=w2_war,
                                 stream_prefix="wa")
    swiglu_pass(tok_w1, tok_w3, tok_w2, None, HS)
    ctx.finish()



def build_l2_attn(stage=4):
    nc = bass.Bass("TRN2", target_bir_lowering=False)
    S, D, I = 4096, 1024, 256
    NS, ND, NI = S // 128, D // 128, I // 128
    x_d = nc.dram_tensor("x", [S, D], F32, kind="ExternalInput").ap()
    wq_d = nc.dram_tensor("wq", [I, D], F32, kind="ExternalInput").ap()
    wk_d = nc.dram_tensor("wk", [I, D], F32, kind="ExternalInput").ap()
    wv_d = nc.dram_tensor("wv", [I, D], F32, kind="ExternalInput").ap()
    wo_d = nc.dram_tensor("wo", [D, I], F32, kind="ExternalInput").ap()
    be_d = nc.dram_tensor("b_eff", [1, D], F32, kind="ExternalInput").ap()
    id_d = nc.dram_tensor("ident", [128, 128], BF16, kind="ExternalInput").ap()
    y_d = nc.dram_tensor("y", [S, D], F32, kind="ExternalOutput").ap()
    ctx = Ctx(nc)
    io = dict(x=x_d, wq=wq_d, wk=wk_d, wv=wv_d, wo=wo_d, b_eff=be_d, ident=id_d,
              y=y_d)
    emit_l2_attn_body(ctx, io, stop_stage=stage)
    with nc.Block() as blk:
        ctx.build(blk)
    ctx.close()
    return nc


def emit_l2_attn_body(ctx, io, stop_stage=4):
    S, D, I, NH, dh = 4096, 1024, 256, 4, 64
    NS, ND, NI = S // 128, D // 128, I // 128

    ident = ctx.sbuf("ident_sb", (128, 128), BF16)
    tok_id = ctx.dma("sync", ident[:], io["ident"])
    tslot = PsumSlot(ctx.psum("tslot_bf", (128, 128), BF16))
    qt_ps = PsumSlot(ctx.psum("qt_ps", (128, 128), F32))
    kv_ps = PsumSlot(ctx.psum("kv_ps", (128, 2 * I), F32))
    ctx_ps = PsumSlot(ctx.psum("ctx_ps", (128, I), F32))
    ao_ps = PsumSlot(ctx.psum("ao_ps", (128, 512), F32))
    y_ps = [PsumSlot(ctx.psum(f"y_ps{i}", (128, 512), F32)) for i in range(2)]

    stage = [ctx.sbuf("wstage0", (128, D), F32), ctx.sbuf("wstage1", (128, D), F32)]
    stok = [None, None]
    wrs = make_wrow_state(ctx, D)
    wqT, tok_q = load_weight_T_bf16(ctx, io["wq"], I, D, "wqT", stage, stok, ident,
                                    tslot, queue="gpsimd", wrow_state=wrs,
                                    extra_waits=(tok_id,))
    wkT, tok_k = load_weight_T_bf16(ctx, io["wk"], I, D, "wkT", stage, stok, ident,
                                    tslot, queue="gpsimd", wrow_state=wrs)
    wvT, tok_v = load_weight_T_bf16(ctx, io["wv"], I, D, "wvT", stage, stok, ident,
                                    tslot, queue="gpsimd", wrow_state=wrs)
    woT, tok_wo = load_weight_T_bf16(ctx, io["wo"], D, I, "woT", stage, stok, ident,
                                     tslot, queue="gpsimd", wrow_state=wrs)
    ones1 = ctx.sbuf("ones1", (1, 128), F32)
    t_o = ctx.op("vector", "memset", ones1[0:1, :], 1.0, inc=True)
    be1 = ctx.sbuf("be1", (1, D), F32)
    tok_be1 = ctx.dma("sync", be1[0:1, :], io["b_eff"])
    bias_bc = ctx.sbuf("bias_bc", (128, D), F32)
    tok_bb = []
    for ci in range(2):
        t = ctx.op("tensor", "matmul", y_ps[ci].tile[:, :512], ones1[0:1, :],
                   be1[0:1, ci * 512 : (ci + 1) * 512], start=True, stop=True,
                   waits=(tok_be1, t_o) + y_ps[ci].war(), inc=True)
        tev = ctx.op("scalar", "copy", bias_bc[:, ci * 512 : (ci + 1) * 512],
                     y_ps[ci].tile[:, :512], waits=(t,), inc=True)
        y_ps[ci].tok = tev
        tok_bb.append(tev)

    if stop_stage <= 0:
        st = ctx.dma("sync", io["y"][0:128, 0:D], bias_bc[:, :D],
                     waits=tuple(tok_bb) + (tok_q, tok_k, tok_v, tok_wo),
                     stream="out")
        ctx.finish()
        return
    qT = [ctx.sbuf(f"qT{m}", (128, S), BF16) for m in range(NI)]
    ksm = [ctx.sbuf(f"ksm{i}", (128, I), BF16) for i in range(NS)]
    vsb = [ctx.sbuf(f"vsb{i}", (128, I), BF16) for i in range(NS)]
    aoT = [ctx.sbuf(f"aoT{m}", (128, S), BF16) for m in range(NI)]
    kmx = ctx.sbuf("kmx", (128, 2), F32)
    ktmp = ctx.sbuf("ktmp", (128, 64), F32)
    kfull = ctx.sbuf("kfull", (128, 256), F32)
    qT_tok = [None] * NS
    kv_tok = [None] * NS
    prev = {}
    with ctx.scope():
        tp = TilePro(ctx, D, ident, tok_id, with_partials=False, tslot=tslot)
        for i in range(NS):
            pro = tp.emit(ctx, i, io["x"], None, None)
            tp.set_xt_done(pro["s"], pro["t_ln"])
            xnT = tp.xnT
            if stop_stage <= 0.7:
                kv_tok[i] = tuple(pro["evs"])
                continue
            for m in range(NI):
                t = emit_dense_mm(
                    ctx, [qt_ps.tile[:, :128]],
                    [wqT[k][:, m * 128 : (m + 1) * 128] for k in range(ND)],
                    [xnT[k][:] for k in range(ND)], [(0, 128)],
                    waits=tuple(pro["evs"]) + (tok_q,) + qt_ps.war())
                ev = ctx.op("scalar", "copy", qT[m][:, i * 128 : (i + 1) * 128],
                            qt_ps.tile[:, :128], waits=(t,), inc=True)
                qt_ps.tok = ev
                qT_tok[i] = ev
            if stop_stage <= 0.9:
                kv_tok[i] = (qT_tok[i],)
                continue
            lh = [xnT[k][:] for k in range(ND)]
            t = emit_dense_mm(ctx, [kv_ps.tile[:, 0:I]], lh, wkT, [(0, I)],
                              waits=tuple(pro["evs"]) + (tok_k,) + kv_ps.war())
            t2 = emit_dense_mm(ctx, [kv_ps.tile[:, I : 2 * I]], lh, wvT, [(0, I)],
                               waits=(tok_v,))
            t_v = ctx.op("scalar", "copy", vsb[i][:, :I], kv_ps.tile[:, I : 2 * I],
                         waits=(t2,), inc=True)
            if stop_stage <= 0.95:
                t_ks = ctx.op("scalar", "copy", ksm[i][:, :I], kv_ps.tile[:, 0:I],
                              waits=(t, t2) + ((prev.get("kmx"),) if prev.get("kmx") else ()),
                              inc=True)
                prev["kmx"] = t_ks
                kv_ps.tok = (t_v, t_ks)
                kv_tok[i] = (t_v, t_ks)
                continue
            t_kc = ctx.op("scalar", "copy", kfull[:, :I], kv_ps.tile[:, 0:I],
                          waits=(t, t2) + ((prev.get("kfull"),) if prev.get("kfull") else ()),
                          inc=True)
            prev["kfull"] = t_kc
            t_ks = None
            for hc in range(NH):
                c0 = hc * 64
                t_m = ctx.op("vector", "reduce_max", kmx[:, 0:1],
                             kfull[:, c0 : c0 + 64], AX.X, negate=True,
                             waits=(t_kc, prev.get("kmx")) if prev.get("kmx")
                             else (t_kc,),
                             inc=True)
                t_e = ctx.op("scalar", "activation", ktmp[:, :64],
                             kfull[:, c0 : c0 + 64], AF.Exp, bias=kmx[:, 0:1],
                             accum_out=kmx[:, 1:2],
                             waits=(t_m,) + ((prev.get("ktmp"),) if prev.get("ktmp") else ()),
                             inc=True)
                t_r = ctx.op("vector", "reciprocal", kmx[:, 1:2], kmx[:, 1:2],
                             waits=(t_e,), inc=True)
                t_ks = ctx.op("vector", "tensor_scalar_mul", ksm[i][:, c0 : c0 + 64],
                              ktmp[:, :64], kmx[:, 1:2], waits=(t_r,), inc=True)
                prev["kmx"] = t_ks
                prev["ktmp"] = t_ks
            kv_ps.tok = (t_v, prev["kfull"])
            kv_tok[i] = (t_v, t_ks)
    ctx.join()
    if stop_stage <= 1:
        dbg = ctx.sbuf("dbg_f32", (128, I), F32)
        srcap = tp.xnT[0][:, :128] if stop_stage <= 0.7 else ksm[0][:, :I]
        c1 = ctx.op("vector", "tensor_copy", dbg[:, : (128 if stop_stage <= 0.7 else I)],
                    srcap, waits=kv_tok[NS - 1], inc=True)
        st = ctx.dma("sync", io["y"][0:128, 0:I], dbg[:, :I], waits=(c1,),
                     stream="out")
        ctx.finish()
        return

    t_ctx = None
    for h in range(NH):
        for i in range(NS):
            t_ctx = ctx.op(
                "tensor", "matmul", ctx_ps.tile[0:64, h * 64 : (h + 1) * 64],
                ksm[i][:, h * 64 : (h + 1) * 64], vsb[i][:, h * 64 : (h + 1) * 64],
                start=(i == 0), stop=(i == NS - 1),
                waits=(kv_tok[i] + ctx_ps.war()) if i == 0 else (),
                inc=(i == NS - 1),
            )
    ctxs = ctx.sbuf("ctxs", (128, I), BF16)
    t_ctxe = []
    for h in range(NH):
        m_base = (h * 64) % 128
        t_ctxe.append(ctx.op("scalar", "copy",
                             ctxs[m_base : m_base + 64, h * 64 : (h + 1) * 64],
                             ctx_ps.tile[0:64, h * 64 : (h + 1) * 64],
                             waits=(t_ctx,), inc=True))
    ctx_ps.tok = tuple(t_ctxe)
    if stop_stage <= 2:
        st = ctx.dma("gpsimd", io["y"][0:128, 0:I], ctxs[:, :I], waits=tuple(t_ctxe),
                     stream="out")
        ctx.finish()
        return

    for h in range(NH):
        m_tile = (h * 64) // 128
        m_base = (h * 64) % 128
        for c in range(S // 512):
            t = ctx.op("tensor", "matmul", ao_ps.tile[0:64, :512],
                       ctxs[m_base : m_base + 64, h * 64 : (h + 1) * 64],
                       qT[m_tile][m_base : m_base + 64, c * 512 : (c + 1) * 512],
                       start=True, stop=True,
                       waits=(t_ctxe[h], qT_tok[-1]) + ao_ps.war(), inc=True)
            ev = ctx.op("scalar", "copy",
                        aoT[m_tile][m_base : m_base + 64, c * 512 : (c + 1) * 512],
                        ao_ps.tile[0:64, :512], waits=(t,), inc=True)
            ao_ps.tok = ev
            prev[("ao", h)] = ev

    if stop_stage <= 3:
        st = ctx.dma("gpsimd", io["y"][0:128, 0:S], aoT[0][:, :S],
                     waits=tuple(prev[("ao", h)] for h in range(NH)), stream="out")
        ctx.finish()
        return
    ao_toks = tuple(prev[("ao", h)] for h in range(NH))
    ysb = ctx.sbuf("ysb", (128, D), F32)
    last_st = None
    for i in range(NS):
        for ci in range(2):
            t = emit_dense_mm(
                ctx, [y_ps[ci].tile[:, :512]],
                [aoT[m][:, i * 128 : (i + 1) * 128] for m in range(NI)],
                woT, [(ci * 512, 512)],
                waits=ao_toks + (tok_wo,) + y_ps[ci].war())
            tev = ctx.op("vector", "tensor_add", ysb[:, ci * 512 : (ci + 1) * 512],
                         y_ps[ci].tile[:, :512],
                         bias_bc[:, ci * 512 : (ci + 1) * 512],
                         waits=(t,) + tuple(tok_bb) + ((last_st,) if last_st else ()),
                         inc=True)
            y_ps[ci].tok = tev
            prev[("y", ci)] = tev
        last_st = ctx.dma("sync", io["y"][i * 128 : (i + 1) * 128, :], ysb[:],
                          waits=(prev[("y", 0)], prev[("y", 1)]), stream="out")
    ctx.finish()



def build_l3_attn():
    nc = bass.Bass("TRN2", target_bir_lowering=False)
    S, D, I = 4096, 1024, 512
    x_d = nc.dram_tensor("x", [S, D], F32, kind="ExternalInput").ap()
    wq_d = nc.dram_tensor("wq", [D, I], F32, kind="ExternalInput").ap()
    wk_d = nc.dram_tensor("wk", [D, I], F32, kind="ExternalInput").ap()
    wv_d = nc.dram_tensor("wv", [D, I], F32, kind="ExternalInput").ap()
    wo_d = nc.dram_tensor("wo", [I, D], F32, kind="ExternalInput").ap()
    be_d = nc.dram_tensor("b_eff", [1, D], F32, kind="ExternalInput").ap()
    pos_d = nc.dram_tensor("posT", [128, S], F32, kind="ExternalInput").ap()
    poss_d = nc.dram_tensor("posT_s", [128, S], F32, kind="ExternalInput").ap()
    msk_d = nc.dram_tensor("cmask", [4 * 128, 512], BF16, kind="ExternalInput").ap()
    id_d = nc.dram_tensor("ident", [128, 128], BF16, kind="ExternalInput").ap()
    id32_d = nc.dram_tensor("ident32", [128, 128], F32, kind="ExternalInput").ap()
    y_d = nc.dram_tensor("y", [S, D], F32, kind="ExternalOutput").ap()
    ctx = Ctx(nc)
    io = dict(x=x_d, wq=wq_d, wk=wk_d, wv=wv_d, wo=wo_d, b_eff=be_d, posT=pos_d,
              posT_s=poss_d, cmask=msk_d, ident=id_d, ident32=id32_d, y=y_d)
    emit_l3_attn_body(ctx, io)
    with nc.Block() as blk:
        ctx.build(blk)
    ctx.close()
    return nc


def emit_l3_attn_body(ctx, io, alibi=False, CONST=1.0):
    S, D, I, NH = 4096, 1024, 512, 8
    NS, ND, NI = S // 128, D // 128, I // 128
    NC_ = S // 512

    ident = ctx.sbuf("ident_sb", (128, 128), F32)
    tok_id = ctx.dma("gpsimd", ident[:], io["ident32"], stream="id")
    tslot = PsumSlot(ctx.psum("tslot_32", (128, 128), F32))
    pj_ps = PsumSlot(ctx.psum("pj_ps", (128, 512), F32))
    sc_ps = [PsumSlot(ctx.psum(f"sc_ps{i}", (128, 512), F32)) for i in range(2)]
    av_ps = PsumSlot(ctx.psum("av_ps", (128, 512), F32))
    y_ps = [PsumSlot(ctx.psum(f"y_ps{i}", (128, 512), F32)) for i in range(2)]


    cmask = ctx.sbuf("cmask_sb", (128, 4 * 512), BF16)
    tok_cm = None
    for b4 in range(4):
        tok_cm = ctx.dma("sync", cmask[:, b4 * 512 : (b4 + 1) * 512],
                         io["cmask"][b4 * 128 : (b4 + 1) * 128, :])
    ones1 = ctx.sbuf("ones1", (1, 128), F32)
    t_o = ctx.op("vector", "memset", ones1[0:1, :], 1.0, inc=True)
    bias_bc = ctx.sbuf("bias_bc", (128, D), BF16)

    qT = [ctx.sbuf(f"qT{m}", (128, S), BF16) for m in range(NI)]
    kT = [ctx.sbuf(f"kT{m}", (128, S), BF16) for m in range(NI)]
    bneg = [ctx.sbuf(f"bneg{h}", (128, 32), F32) for h in range(NH)]
    ebv = [ctx.sbuf(f"ebv{h}", (128, 32), F32) for h in range(NH)]
    vpp = [ctx.sbuf(f"vpp{h}", (128, 32 * 65), BF16) for h in range(NH)]
    vpp_tok = [None] * NH
    bneg_tok = [None] * NH
    ebv_tok = [None] * NH
    onesc = ctx.sbuf("onesc", (128, 1), BF16)
    t_oc = ctx.op("vector", "memset", onesc[:, 0:1], 1.0, inc=True)
    nrm_ps = PsumSlot(ctx.psum("nrm_ps", (128, 512), F32))
    qT_tok = [None] * NI
    kT_tok = [None] * NI
    v_tok = [None] * NS
    prev = {}

    with ctx.scope():
        stage = [ctx.sbuf("wstage0", (128, D), F32)]
        stok = [None]
        wq, tok_wq = load_weight_bf16(ctx, io["wq"], D, I, "wq", stage, stok, "gpsimd")
        wk, tok_wk = load_weight_bf16(ctx, io["wk"], D, I, "wk", stage, stok, "gpsimd")
        wv, tok_wv = load_weight_bf16(ctx, io["wv"], D, I, "wv", stage, stok, "gpsimd")
        tok_pos = None
        if not alibi:
            posT = ctx.sbuf("posT_sb", (128, S), BF16)
            posTs = ctx.sbuf("posTs_sb", (128, S), BF16)
            tok_p1 = ctx.dma("gpsimd", posT[:], io["posT"], stream="pos")
            tok_p2 = ctx.dma("gpsimd", posTs[:], io["posT_s"], stream="pos")
            tok_pos = (tok_p1, tok_p2)
        vsb = [ctx.sbuf(f"vsb{i}", (128, I), BF16) for i in range(NS)]
        tp = TilePro(ctx, D, ident, tok_id, with_partials=False, tslot=tslot,
                     n_xt=1, direct32=True)
        for i in range(NS):
            pro = tp.emit(ctx, i, io["x"], None, None)
            tp.set_xt_done(pro["s"], pro["t_ln"])
            xnT = tp.xnT
            col = slice(i * 128, (i + 1) * 128)
            for (wt, dst, toks, tokw) in ((wq, qT, qT_tok, tok_wq),
                                          (wk, kT, kT_tok, tok_wk)):
                for m in range(NI):
                    t = emit_dense_mm(
                        ctx, [pj_ps.tile[:, :128]],
                        [wt[k][:, m * 128 : (m + 1) * 128] for k in range(ND)],
                        [xnT[k][:] for k in range(ND)], [(0, 128)],
                        waits=tuple(pro["evs"]) + (tokw,) + pj_ps.war())
                    ev = ctx.op("scalar", "copy", dst[m][:, col],
                                pj_ps.tile[:, :128], waits=(t,), inc=True)
                    pj_ps.tok = ev
                    toks[m] = ev
            t = emit_dense_mm(ctx, [pj_ps.tile[:, :I]], [xnT[k][:] for k in range(ND)],
                              wv, [(0, I)], waits=tuple(pro["evs"]) + (tok_wv,)
                              + pj_ps.war())
            v_tok[i] = ctx.op("scalar", "copy", vsb[i][:, :I], pj_ps.tile[:, :I],
                              waits=(t,), inc=True)
            pj_ps.tok = v_tok[i]
        q_toks, k_toks = [], []
        for m in range(NI):
            if alibi:
                tq = ctx.op("vector", "tensor_scalar_mul", qT[m][:, :S],
                            qT[m][:, :S], 0.125, waits=(qT_tok[m],), inc=True)
                tk = ctx.op("vector", "engine_nop", waits=(kT_tok[m],), inc=True)
            else:
                tq = ctx.op("vector", "scalar_tensor_tensor", qT[m][:, :S],
                            qT[m][:, :S], 1.0 / 32.0, posTs[:, :S], ALU.mult,
                            ALU.add,
                            waits=(qT_tok[m],) + ((tok_pos,) if tok_pos else ()),
                            inc=True)
                tk = ctx.op("vector", "tensor_add", kT[m][:, :S], kT[m][:, :S],
                            posT[:, :S], waits=(kT_tok[m],), inc=True)
            q_toks.append(tq)
            k_toks.append(tk)

        ksq = ctx.sbuf("ksq", (128, 512), BF16)
        nrow = ctx.sbuf("nrow", (1, 512), F32)
        for h in range(NH):
            mt, mb = h // 2, (h % 2) * 64
            t_nr = None
            for c in range(NC_):
                t_sq = ctx.op("scalar", "activation", ksq[:, :512],
                              kT[mt][:, c * 512 : (c + 1) * 512], AF.Square,
                              waits=(k_toks[mt],)
                              + ((prev.get("ksq"),) if prev.get("ksq") else ()),
                              inc=True)
                t1 = ctx.op("tensor", "matmul", nrm_ps.tile[:1, :512],
                            onesc[mb : mb + 64, 0:1],
                            ksq[mb : mb + 64, :512],
                            start=True, stop=True,
                            waits=(t_sq, t_oc) + nrm_ps.war(), inc=True)
                prev["ksq"] = t1
                t_nr = ctx.op("scalar", "copy", nrow[0:1, :512],
                              nrm_ps.tile[0:1, :512],
                              waits=(t1,) + ((prev.get("nrow"),) if prev.get("nrow") else ()),
                              inc=True)
                nrm_ps.tok = t_nr
                tts = []
                for j4 in range(4):
                    jt = c * 4 + j4
                    tt = ctx.op("tensor", "transpose", tslot.tile[:, 0:1],
                                nrow[0:1, j4 * 128 : (j4 + 1) * 128],
                                ident[0:1, 0:1],
                                waits=(t_nr,) + tslot.war(), inc=True)
                    ev = ctx.op("scalar", "copy", bneg[h][:, jt : jt + 1],
                                tslot.tile[:, 0:1], waits=(tt,), inc=True)
                    tslot.tok = ev
                    tts.append(ev)
                prev["nrow"] = tts[-1]
            t_b = ctx.op("scalar", "activation", bneg[h][:, :32], bneg[h][:, :32],
                         AF.Sqrt, waits=tuple(tts), inc=True)
            t_b = ctx.op("vector", "tensor_scalar", bneg[h][:, :32], bneg[h][:, :32],
                         -CONST, -60.0, ALU.mult, ALU.max, waits=(t_b,), inc=True)
            bneg_tok[h] = t_b
            t_e = ctx.op("scalar", "activation", ebv[h][:, :32], bneg[h][:, :32],
                         AF.Exp, scale=-1.0, waits=(t_b,), inc=True)
            ebv_tok[h] = t_e

        for h in range(NH):
            tlast = None
            for jt in range(NS):
                c0 = jt * 65
                t1 = ctx.op("vector", "tensor_scalar_mul", vpp[h][:, c0 : c0 + 64],
                            vsb[jt][:, h * 64 : (h + 1) * 64],
                            ebv[h][:, jt : jt + 1],
                            waits=(v_tok[jt], ebv_tok[h]), inc=True)
                t2 = ctx.op("vector", "tensor_copy", vpp[h][:, c0 + 64 : c0 + 65],
                            ebv[h][:, jt : jt + 1], waits=(t1,), inc=True)
                tlast = t2
            vpp_tok[h] = tlast
    ctx.join()
    aoT = [ctx.sbuf(f"aoT{m}", (128, S), BF16) for m in range(NI)]
    wo = [ctx.sbuf(f"wo_k{k}", (128, D), BF16) for k in range(NI)]
    stage2 = [ctx.sbuf("wstage2", (128, D), F32)]
    stok2 = [None]
    ones1b = ctx.sbuf("ones1b", (1, 128), BF16)
    t_ob = ctx.op("vector", "memset", ones1b[0:1, :], 1.0, inc=True)
    be1 = ctx.sbuf("be1", (1, D), BF16)
    tok_be1 = ctx.dma("gpsimd", be1[0:1, :], io["b_eff"], stream="pos")
    tok_bb = []
    for ci in range(2):
        t = ctx.op("tensor", "matmul", y_ps[ci].tile[:, :512], ones1b[0:1, :],
                   be1[0:1, ci * 512 : (ci + 1) * 512], start=True, stop=True,
                   waits=(tok_be1, t_ob) + y_ps[ci].war(), inc=True)
        tev = ctx.op("scalar", "copy", bias_bc[:, ci * 512 : (ci + 1) * 512],
                     y_ps[ci].tile[:, :512], waits=(t,), inc=True)
        y_ps[ci].tok = tev
        tok_bb.append(tev)
    _, tok_wo = load_weight_bf16(ctx, io["wo"], I, D, "wo", stage2, stok2, "gpsimd",
                                 out_tiles=wo)

    psb = [ctx.sbuf(f"psb{i}", (128, 512), BF16) for i in range(2)]
    rb = ctx.sbuf("rb", (128, 512), BF16)
    rrow = ctx.sbuf("rrow", (1, 512), F32)
    psb_tok = [None, None]
    if alibi:
        balibi = ctx.sbuf("balibi_sh", (128, 1024), F32)
        aramp_buf = ctx.sbuf("aramp_buf", (128, 1024), F32)
        balibi_tok = None
    for h in range(NH):
        mt, mb = h // 2, (h % 2) * 64
        if alibi:
            t_ar = ctx.dma("sync", aramp_buf[:, :], io["aramp"][h], stream="ar",
                           waits=(balibi_tok,) if balibi_tok else ())
            tb2 = None
            war_exp = (prev.get("balibi_last_exp"),) if prev.get("balibi_last_exp") else ()
            for jt in range(NS):
                tb2 = ctx.op("vector", "tensor_scalar",
                             balibi[:, jt * 32 : (jt + 1) * 32],
                             aramp_buf[:, jt * 32 : (jt + 1) * 32],
                             bneg[h][:, jt : jt + 1], 72.0, ALU.add, ALU.min,
                             waits=(bneg_tok[h], t_ar)
                             + (war_exp if jt == 0 else ())
                             + ((balibi_tok,) if balibi_tok and jt == 0 else ()),
                             inc=True)
            balibi_tok = tb2
        for c in range(NC_):
            njt = 4 * c + 4
            av_started = False
            t_av = None
            for jt in range(njt):
                sl = jt % 2
                sp_ = sc_ps[sl]
                t_s = ctx.op(
                    "tensor", "matmul", sp_.tile[:, :512],
                    kT[mt][mb : mb + 64, jt * 128 : (jt + 1) * 128],
                    qT[mt][mb : mb + 64, c * 512 : (c + 1) * 512],
                    start=True, stop=True,
                    waits=(k_toks[mt], q_toks[mt]) + sp_.war(), inc=True)
                if alibi:
                    t_p = None
                    for qb in range(4):
                        t_p = ctx.op(
                            "scalar", "activation",
                            psb[sl][:, qb * 128 : (qb + 1) * 128],
                            sp_.tile[:, qb * 128 : (qb + 1) * 128], AF.Exp,
                            bias=balibi[:, jt * 32 + (c * 4 + qb) : jt * 32 + (c * 4 + qb) + 1],
                            waits=(t_s, balibi_tok)
                            + ((psb_tok[sl],) if psb_tok[sl] and qb == 0 else ()),
                            inc=(qb == 3))
                else:
                    t_p = ctx.op("scalar", "activation", psb[sl][:, :512],
                                 sp_.tile[:, :512], AF.Exp,
                                 bias=bneg[h][:, jt : jt + 1],
                                 waits=(t_s, bneg_tok[h])
                                 + ((psb_tok[sl],) if psb_tok[sl] else ()),
                                 inc=True)
                sp_.tok = t_p
                prev["balibi_last_exp"] = t_p
                band = jt - 4 * c
                if band >= 0:
                    t_p = ctx.op("vector", "tensor_tensor", psb[sl][:, :512],
                                 psb[sl][:, :512],
                                 cmask[:, band * 512 : (band + 1) * 512], ALU.mult,
                                 waits=(t_p, tok_cm), inc=True)
                t_av = ctx.op(
                    "tensor", "matmul", av_ps.tile[:65, :512],
                    vpp[h][:, jt * 65 : (jt + 1) * 65], psb[sl][:, :512],
                    start=(jt == 0), stop=(jt == njt - 1),
                    waits=(t_p, vpp_tok[h]) + (av_ps.war() if jt == 0 else ()),
                    inc=(jt == njt - 1))
                psb_tok[sl] = t_av
            t_r = ctx.op("vector", "reciprocal", rrow[0:1, :512],
                         av_ps.tile[64:65, :512],
                         waits=(t_av,) + ((prev.get("rrow"),) if prev.get("rrow") else ()),
                         inc=True)
            t_rb = ctx.op("tensor", "matmul", sc_ps[0].tile[:, :512], ones1[0:1, :],
                          rrow[0:1, :512], start=True, stop=True,
                          waits=(t_r,) + sc_ps[0].war(), inc=True)
            t_rbe = ctx.op("scalar", "copy", rb[:, :512], sc_ps[0].tile[:, :512],
                           waits=(t_rb,) + ((prev.get("rb"),) if prev.get("rb") else ()),
                           inc=True)
            sc_ps[0].tok = t_rbe
            psb_tok[0] = t_rbe
            t_ao = ctx.op("vector", "tensor_tensor",
                          aoT[mt][mb : mb + 64, c * 512 : (c + 1) * 512],
                          av_ps.tile[0:64, :512], rb[mb : mb + 64, :512], ALU.mult,
                          waits=(t_rbe, t_av), inc=True)
            av_ps.tok = t_ao
            prev["rrow"] = t_ao
            prev["rb"] = t_ao
            prev[("aoT", mt)] = t_ao

    ao_toks = tuple(prev[("aoT", m)] for m in range(NI) if ("aoT", m) in prev)
    ysb = ctx.sbuf("ysb", (128, D), F32)
    last_st = None
    for i in range(NS):
        for ci in range(2):
            t = emit_dense_mm(
                ctx, [y_ps[ci].tile[:, :512]],
                [aoT[m][:, i * 128 : (i + 1) * 128] for m in range(NI)],
                wo, [(ci * 512, 512)],
                waits=ao_toks + (tok_wo,) + y_ps[ci].war())
            tev = ctx.op("vector", "tensor_add", ysb[:, ci * 512 : (ci + 1) * 512],
                         y_ps[ci].tile[:, :512],
                         bias_bc[:, ci * 512 : (ci + 1) * 512],
                         waits=(t,) + tuple(tok_bb) + ((last_st,) if last_st else ()),
                         inc=True)
            y_ps[ci].tok = tev
            prev[("y", ci)] = tev
        last_st = ctx.dma("sync", io["y"][i * 128 : (i + 1) * 128, :], ysb[:],
                          waits=(prev[("y", 0)], prev[("y", 1)]), stream="out")
    ctx.finish()


def build_l0_attn(head_base):
    nc = bass.Bass("TRN2", target_bir_lowering=False)
    S, D, I = 4096, 1024, 512
    x_d = nc.dram_tensor("x", [S, D], F32, kind="ExternalInput").ap()
    wq_d = nc.dram_tensor("wq", [D, I], F32, kind="ExternalInput").ap()
    wk_d = nc.dram_tensor("wk", [D, I], F32, kind="ExternalInput").ap()
    wv_d = nc.dram_tensor("wv", [D, I], F32, kind="ExternalInput").ap()
    wo_d = nc.dram_tensor("wo", [I, D], F32, kind="ExternalInput").ap()
    be_d = nc.dram_tensor("b_eff", [1, D], F32, kind="ExternalInput").ap()
    ar_d = nc.dram_tensor("aramp", [8, 128, 1024], F32, kind="ExternalInput").ap()
    msk_d = nc.dram_tensor("cmask", [4 * 128, 512], BF16, kind="ExternalInput").ap()
    id_d = nc.dram_tensor("ident", [128, 128], BF16, kind="ExternalInput").ap()
    id32_d = nc.dram_tensor("ident32", [128, 128], F32, kind="ExternalInput").ap()
    y_d = nc.dram_tensor("y", [S, D], F32, kind="ExternalOutput").ap()
    ctx = Ctx(nc)
    io = dict(x=x_d, wq=wq_d, wk=wk_d, wv=wv_d, wo=wo_d, b_eff=be_d, aramp=ar_d,
              cmask=msk_d, ident=id_d, ident32=id32_d, y=y_d)
    emit_l3_attn_body(ctx, io, alibi=True, CONST=2.5)
    with nc.Block() as blk:
        ctx.build(blk)
    ctx.close()
    return nc


def l0_attn_aramp(head_base):
    H = 16
    slopes = (2.0 ** (-8.0 / H)) ** (np.arange(1, H + 1))
    out = np.zeros((8, 128, 1024), np.float64)
    dj = np.arange(128)
    for h in range(8):
        sl = slopes[head_base + h]
        for jt in range(32):
            for qb in range(32):
                out[h][:, jt * 32 + qb] = sl * (jt * 128 + dj - (qb * 128 + 64))
    return out.astype(np.float32)



def build_l1_attn():
    nc = bass.Bass("TRN2", target_bir_lowering=False)
    S, D, I = 4096, 1024, 512
    x_d = nc.dram_tensor("x", [S, D], F32, kind="ExternalInput").ap()
    wq_d = nc.dram_tensor("wq", [D, I], F32, kind="ExternalInput").ap()
    wk_d = nc.dram_tensor("wk", [D, I], F32, kind="ExternalInput").ap()
    wv_d = nc.dram_tensor("wv", [D, I], F32, kind="ExternalInput").ap()
    wo_d = nc.dram_tensor("wo", [I, D], F32, kind="ExternalInput").ap()
    bq_d = nc.dram_tensor("bq", [I, 1], F32, kind="ExternalInput").ap()
    bk_d = nc.dram_tensor("bk", [I, 1], F32, kind="ExternalInput").ap()
    bv_d = nc.dram_tensor("bv", [I, 1], F32, kind="ExternalInput").ap()
    bvr_d = nc.dram_tensor("bv_row", [1, I], F32, kind="ExternalInput").ap()
    be_d = nc.dram_tensor("b_eff", [1, D], F32, kind="ExternalInput").ap()
    cos_d = nc.dram_tensor("cosT", [128, S], F32, kind="ExternalInput").ap()
    sin_d = nc.dram_tensor("sinT", [128, S], F32, kind="ExternalInput").ap()
    psw_d = nc.dram_tensor("pswapT", [128, 128], BF16, kind="ExternalInput").ap()
    id32_d = nc.dram_tensor("ident32", [128, 128], F32, kind="ExternalInput").ap()
    y_d = nc.dram_tensor("y", [S, D], F32, kind="ExternalOutput").ap()
    ctx = Ctx(nc)
    io = dict(x=x_d, wq=wq_d, wk=wk_d, wv=wv_d, wo=wo_d, bq=bq_d, bk=bk_d,
              bv=bv_d, bv_row=bvr_d, b_eff=be_d, cosT=cos_d, sinT=sin_d,
              pswapT=psw_d, ident32=id32_d, y=y_d)
    emit_l1_attn_body(ctx, io)
    with nc.Block() as blk:
        ctx.build(blk)
    ctx.close()
    return nc


def emit_l1_attn_body(ctx, io):
    S, D, I, NH = 4096, 1024, 512, 8
    NS, ND, NI = S // 128, D // 128, I // 128
    NC_ = S // 512

    ident = ctx.sbuf("ident_sb", (128, 128), F32)
    tok_id = ctx.dma("gpsimd", ident[:], io["ident32"], stream="id")
    tslot = PsumSlot(ctx.psum("tslot_32", (128, 128), F32))
    pj_ps = PsumSlot(ctx.psum("pj_ps", (128, 512), F32))
    num_ps = [PsumSlot(ctx.psum(f"num_ps{i}", (128, 512), F32)) for i in range(2)]
    den_ps = PsumSlot(ctx.psum("den_ps", (128, 512), F32))
    av_ps = PsumSlot(ctx.psum("av_ps", (128, 512), F32))
    y_ps = [PsumSlot(ctx.psum(f"y_ps{i}", (128, 512), F32)) for i in range(2)]

    qT = [ctx.sbuf(f"qT{m}", (128, S), BF16) for m in range(NI)]
    kT = [ctx.sbuf(f"kT{m}", (128, S), BF16) for m in range(NI)]
    rqT = [ctx.sbuf(f"rqT{m}", (128, S), BF16) for m in range(NI)]
    rkT = [ctx.sbuf(f"rkT{m}", (128, S), BF16) for m in range(NI)]
    vsb = [ctx.sbuf(f"vsb{i}", (128, I), BF16) for i in range(NS)]
    bqkv = ctx.sbuf("bqkv", (128, 3 * NI), F32)
    qT_tok = [None] * NI
    kT_tok = [None] * NI
    v_tok = [None] * NS
    prev = {}

    tok_b = []
    for j, nm in enumerate(("bq", "bk", "bv")):
        for m in range(NI):
            tok_b.append(ctx.dma("sync", bqkv[:, j * NI + m : j * NI + m + 1],
                                 io[nm][m * 128 : (m + 1) * 128, :], stream="b"))
    tok_b = tuple(tok_b)

    ones1v = ctx.sbuf("ones1v", (1, 128), BF16)
    t_ov = ctx.op("vector", "memset", ones1v[0:1, :], 1.0, inc=True)
    bvr = ctx.sbuf("bvr", (1, I), BF16)
    tok_bvr = ctx.dma("gpsimd", bvr[0:1, :], io["bv_row"], stream="b2")
    bvrow_bc = ctx.sbuf("bvrow_bc", (128, I), BF16)
    t = ctx.op("tensor", "matmul", pj_ps.tile[:, :I], ones1v[0:1, :], bvr[0:1, :I],
               start=True, stop=True, waits=(tok_bvr, t_ov) + pj_ps.war(), inc=True)
    tok_bvb = ctx.op("scalar", "copy", bvrow_bc[:, :I], pj_ps.tile[:, :I],
                     waits=(t,), inc=True)
    pj_ps.tok = tok_bvb

    with ctx.scope():
        stage = [ctx.sbuf("wstage0", (128, D), F32)]
        stok = [None]
        wq, tok_wq = load_weight_bf16(ctx, io["wq"], D, I, "wq", stage, stok, "gpsimd")
        wk, tok_wk = load_weight_bf16(ctx, io["wk"], D, I, "wk", stage, stok, "gpsimd")
        wv, tok_wv = load_weight_bf16(ctx, io["wv"], D, I, "wv", stage, stok, "gpsimd")
        tp = TilePro(ctx, D, ident, tok_id, with_partials=False, tslot=tslot,
                     n_xt=1, direct32=True)
        for i in range(NS):
            pro = tp.emit(ctx, i, io["x"], None, None)
            tp.set_xt_done(pro["s"], pro["t_ln"])
            xnT = tp.xnT
            col = slice(i * 128, (i + 1) * 128)
            for (wt, dst, toks, tokw) in ((wq, qT, qT_tok, tok_wq),
                                          (wk, kT, kT_tok, tok_wk)):
                for m in range(NI):
                    t = emit_dense_mm(
                        ctx, [pj_ps.tile[:, :128]],
                        [wt[k][:, m * 128 : (m + 1) * 128] for k in range(ND)],
                        [xnT[k][:] for k in range(ND)], [(0, 128)],
                        waits=tuple(pro["evs"]) + (tokw,) + pj_ps.war())
                    ev = ctx.op("scalar", "copy", dst[m][:, col],
                                pj_ps.tile[:, :128], waits=(t,), inc=True)
                    pj_ps.tok = ev
                    toks[m] = ev
            t = emit_dense_mm(ctx, [pj_ps.tile[:, :I]], [xnT[k][:] for k in range(ND)],
                              wv, [(0, I)], waits=tuple(pro["evs"]) + (tok_wv,)
                              + pj_ps.war())
            v_tok[i] = ctx.op("vector", "tensor_add", vsb[i][:, :I],
                              pj_ps.tile[:, :I], bvrow_bc[:, :I],
                              waits=(t, tok_bvb), inc=True)
            pj_ps.tok = v_tok[i]
    ctx.join()

    wo = [ctx.sbuf(f"wo_k{k}", (128, D), BF16) for k in range(NI)]
    stage2 = [ctx.sbuf("wstage2", (128, D), F32)]
    stok2 = [None]
    _, tok_wo = load_weight_bf16(ctx, io["wo"], I, D, "wo", stage2, stok2, "gpsimd",
                                 out_tiles=wo)
    ysb = ctx.sbuf("ysb", (128, 512), F32)
    ones1b = ctx.sbuf("ones1b", (1, 128), BF16)
    t_ob = ctx.op("vector", "memset", ones1b[0:1, :], 1.0, inc=True)
    be1 = ctx.sbuf("be1", (1, D), BF16)
    tok_be1 = ctx.dma("gpsimd", be1[0:1, :], io["b_eff"], stream="pos")
    bias_bc = ctx.sbuf("bias_bc", (128, D), BF16)
    tok_bb = []
    for ci in range(2):
        t = ctx.op("tensor", "matmul", y_ps[ci].tile[:, :512], ones1b[0:1, :],
                   be1[0:1, ci * 512 : (ci + 1) * 512], start=True, stop=True,
                   waits=(tok_be1, t_ob) + y_ps[ci].war(), inc=True)
        tev = ctx.op("scalar", "copy", bias_bc[:, ci * 512 : (ci + 1) * 512],
                     y_ps[ci].tile[:, :512], waits=(t,), inc=True)
        y_ps[ci].tok = tev
        tok_bb.append(tev)

    phase2_scope = ctx.scope()
    phase2_scope.__enter__()
    cosT = ctx.sbuf("cosT_sb", (128, S), BF16)
    sinT = ctx.sbuf("sinT_sb", (128, S), BF16)
    t_cs = ctx.dma("gpsimd", cosT[:], io["cosT"], stream="cs")
    t_sn = ctx.dma("gpsimd", sinT[:], io["sinT"], stream="cs")
    psw = ctx.sbuf("psw_sb", (128, 128), BF16)
    t_psw = ctx.dma("sync", psw[:], io["pswapT"], stream="b")
    tmp = ctx.sbuf("tmp_sb", (128, S // 2), BF16)
    tmp2 = ctx.sbuf("tmp2_sb", (128, 512), BF16)
    q_toks = [None] * NI
    k_toks = [None] * NI
    rq_toks = [None] * NI
    rk_toks = [None] * NI
    for m in range(NI):
        for (src_t, src_tok, rdst, rtoks, dtoks, bcol) in (
                (qT, qT_tok, rqT, rq_toks, q_toks, m),
                (kT, kT_tok, rkT, rk_toks, k_toks, NI + m)):
            t0 = ctx.op("vector", "tensor_scalar_add", src_t[m][:, :S],
                        src_t[m][:, :S], bqkv[:, bcol : bcol + 1],
                        waits=(src_tok[m],) + tok_b, inc=True)
            t3 = None
            for hf in range(2):
                sh = slice(hf * (S // 2), (hf + 1) * (S // 2))
                t1 = ctx.op("vector", "tensor_scalar_min", tmp[:, : S // 2],
                            src_t[m][:, sh], 0.0,
                            waits=(t0,) + ((prev.get("tmp"),) if prev.get("tmp") else ()),
                            inc=True)
                t2 = ctx.op("scalar", "activation", tmp[:, : S // 2],
                            tmp[:, : S // 2], AF.Exp, waits=(t1,), inc=True)
                t3 = ctx.op("vector", "scalar_tensor_tensor", src_t[m][:, sh],
                            src_t[m][:, sh], 0.0, tmp[:, : S // 2], ALU.max,
                            ALU.add, waits=(t2,), inc=True)
                prev["tmp"] = t3
            dtoks[m] = t3
            tr = None
            for c in range(NC_):
                sl_ = slice(c * 512, (c + 1) * 512)
                tm = ctx.op("tensor", "matmul", pj_ps.tile[:, :512], psw[:],
                            src_t[m][:, sl_], start=True, stop=True,
                            waits=(t3, t_psw) + pj_ps.war(), inc=True)
                ta = ctx.op("vector", "tensor_tensor", rdst[m][:, sl_],
                            src_t[m][:, sl_], cosT[:, sl_], ALU.mult,
                            waits=(t3, t_cs)
                            + ((prev.get("rr"),) if prev.get("rr") else ()),
                            inc=True)
                tb = ctx.op("vector", "tensor_tensor", tmp2[:, :512],
                            pj_ps.tile[:, :512], sinT[:, sl_], ALU.mult,
                            waits=(tm, t_sn, t3)
                            + ((prev.get("tmp2"),) if prev.get("tmp2") else ()),
                            inc=True)
                pj_ps.tok = tb
                tr = ctx.op("vector", "tensor_add", rdst[m][:, sl_],
                            rdst[m][:, sl_], tmp2[:, :512], waits=(ta, tb), inc=True)
                prev["tmp2"] = tr
                prev["rr"] = tr
            rtoks[m] = tr

    phase2_scope.__exit__(None, None, None)
    ctx.join()

    psb = [ctx.sbuf("psb0", (128, 512), BF16)] * 2
    rden = ctx.sbuf("rden", (128, 512), F32)
    aoT = [ctx.sbuf("aoT0", (128, S), BF16), qT[0], qT[1], qT[2]]
    psb_tok = [None, None]
    for h in range(NH):
        mt, mb = h // 2, (h % 2) * 64
        for c in range(NC_):
            t_av = None
            for jt in range(NS):
                sl = jt % 2
                np_ = num_ps[sl]
                t_n = ctx.op(
                    "tensor", "matmul", np_.tile[:, :512],
                    rkT[mt][mb : mb + 64, jt * 128 : (jt + 1) * 128],
                    rqT[mt][mb : mb + 64, c * 512 : (c + 1) * 512],
                    start=True, stop=True,
                    waits=(rk_toks[mt], rq_toks[mt]) + np_.war(), inc=True)
                t_d = ctx.op(
                    "tensor", "matmul", den_ps.tile[:, :512],
                    kT[mt][mb : mb + 64, jt * 128 : (jt + 1) * 128],
                    qT[mt][mb : mb + 64, c * 512 : (c + 1) * 512],
                    start=True, stop=True,
                    waits=(k_toks[mt], q_toks[mt]) + den_ps.war(), inc=True)
                t_r = ctx.op("vector", "reciprocal", rden[:, :512],
                             den_ps.tile[:, :512],
                             waits=(t_d,) + ((prev.get("rden"),) if prev.get("rden") else ()),
                             inc=True)
                den_ps.tok = t_r
                t_w = ctx.op("vector", "tensor_tensor", psb[sl][:, :512],
                             np_.tile[:, :512], rden[:, :512], ALU.mult,
                             waits=(t_n, t_r)
                             + ((psb_tok[sl],) if psb_tok[sl] else ()),
                             inc=True)
                np_.tok = t_w
                prev["rden"] = t_w
                t_av = ctx.op(
                    "tensor", "matmul", av_ps.tile[:64, :512],
                    vsb[jt][:, h * 64 : (h + 1) * 64], psb[sl][:, :512],
                    start=(jt == 0), stop=(jt == NS - 1),
                    waits=(t_w, v_tok[jt]) + (av_ps.war() if jt == 0 else ()),
                    inc=(jt == NS - 1))
                psb_tok[sl] = t_av
            t_ao = ctx.op("scalar", "copy",
                          aoT[mt][mb : mb + 64, c * 512 : (c + 1) * 512],
                          av_ps.tile[0:64, :512], waits=(t_av,), inc=True)
            av_ps.tok = t_ao
            prev[("aoT", mt)] = t_ao

    ao_toks = tuple(prev[("aoT", m)] for m in range(NI) if ("aoT", m) in prev)
    last_st = None
    for i in range(NS):
        for ci in range(2):
            t = emit_dense_mm(
                ctx, [y_ps[ci].tile[:, :512]],
                [aoT[m][:, i * 128 : (i + 1) * 128] for m in range(NI)],
                wo, [(ci * 512, 512)],
                waits=ao_toks + (tok_wo,) + y_ps[ci].war())
            tev = ctx.op("vector", "tensor_add", ysb[:, :512],
                         y_ps[ci].tile[:, :512],
                         bias_bc[:, ci * 512 : (ci + 1) * 512],
                         waits=(t,) + tuple(tok_bb) + ((last_st,) if last_st else ()),
                         inc=True)
            y_ps[ci].tok = tev
            last_st = ctx.dma("sync",
                              io["y"][i * 128 : (i + 1) * 128,
                                      ci * 512 : (ci + 1) * 512],
                              ysb[:, :512], waits=(tev,), stream="out")
    ctx.finish()


def l1_attn_consts():
    S, hd = 4096, 64
    half = hd // 2
    theta = 1.0 / 10000 ** (np.arange(half, dtype=np.float64) / half)
    ang = np.arange(S, dtype=np.float64)[:, None] * theta[None, :]
    cos = np.repeat(np.cos(ang), 2, axis=1).T
    sin = np.repeat(np.sin(ang), 2, axis=1).T
    cosT = np.concatenate([cos, cos], 0).astype(np.float32)
    sinT = np.concatenate([sin, sin], 0).astype(np.float32)
    P = np.zeros((128, 128), np.float32)
    for i in range(64):
        P[2 * i, 2 * i + 1] = -1.0
        P[2 * i + 1, 2 * i] = 1.0
    return dict(cosT=cosT, sinT=sinT, pswapT=np_bf16(P.T),
                ident32=np.eye(128, dtype=np.float32))


def l3_attn_consts():
    S, hd = 4096, 64
    EPS = float(np.finfo(np.float32).eps)
    idx = np.arange(S, dtype=np.float64)
    sc = 1.0 / (10000.0 ** (np.arange(0, hd, 2, dtype=np.float64) / hd) + EPS)
    ang = idx[:, None] * sc[None, :]
    pos = np.stack([np.sin(ang), np.cos(ang)], axis=-1).reshape(S, hd)
    posT = pos.T.astype(np.float32)
    posT2 = np.concatenate([posT, posT], axis=0)
    m = np.zeros((4, 128, 512), np.float32)
    for b in range(4):
        j = b * 128 + np.arange(128)[:, None]
        q = np.arange(512)[None, :]
        m[b] = (j <= q).astype(np.float32)
    return dict(posT=np.ascontiguousarray(posT2),
                posT_s=np.ascontiguousarray(posT2 / 32.0),
                cmask=np_bf16(m.reshape(4 * 128, 512)),
                ident=identity_bf16(),
                ident32=np.eye(128, dtype=np.float32))


def l2_ffn_consts():
    D = 1024
    n = np.arange(D)
    ang = 2.0 * np.pi * np.outer(n, n).astype(np.float64) / D
    Fr = np.cos(ang)
    Fi = -np.sin(ang)
    Gr = Fr / D
    Gi = -Fi / D
    return dict(
        fr=np_bf16(Fr), nfi=np_bf16(-Fi), gr=np_bf16(Gr), gi=np_bf16(Gi),
        ngi=np_bf16(-Gi), ident=identity_bf16(),
    )
